# version 24
# baseline (speedup 1.0000x reference)
import math
import contextlib
import numpy as np

import concourse.bass as bass
import concourse.mybir as mybir
from concourse.bass_utils import run_bass_kernel_spmd

F32 = mybir.dt.float32
BF16 = mybir.dt.bfloat16
AF = mybir.ActivationFunctionType
ALU = mybir.AluOpType
AXX = mybir.AxisListType.X

C = 1024
CC = 8
TSEQ = 4096
NH = 16
HPAIRS = 8
EXP = 8
NE = 9
H = 2048
HHN = 16
PAD = 2048
TL = PAD + TSEQ
KT_MAX = 44
TT_N = 11
OWN = 4
OWNT = 2048
SLOPES = [(h + 1) / NH for h in range(NH)]
WT = [min(16, math.ceil(14.75 / (h + 1))) for h in range(NH)]
NEG = -30000.0


def build_nc(stage="full"):
    nc = bass.Bass("TRN2", target_bir_lowering=False)

    dx = nc.dram_tensor("x", [128, TL // 512, CC, 512], F32, kind="ExternalInput").ap()
    dvalid = nc.dram_tensor("valid", [128, 48], F32, kind="ExternalInput").ap()
    dqw = nc.dram_tensor("qw", [HPAIRS, 128, CC, 128], F32, kind="ExternalInput").ap()
    dqb = nc.dram_tensor("qb", [128, 8], F32, kind="ExternalInput").ap()
    dkvw = nc.dram_tensor("kvw", [HPAIRS, 128, CC, 256], F32, kind="ExternalInput").ap()
    dbvr = nc.dram_tensor("bvr", [1, C], F32, kind="ExternalInput").ap()
    dow = nc.dram_tensor("ow", [128, CC, C], F32, kind="ExternalInput").ap()
    dob = nc.dram_tensor("ob", [128, 8], F32, kind="ExternalInput").ap()
    dupt = nc.dram_tensor("upt", [NE, HHN, 128, CC, 128], F32, kind="ExternalInput").ap()
    dgat = nc.dram_tensor("gat", [NE, HHN, 128, CC, 128], F32, kind="ExternalInput").ap()
    ddnt = nc.dram_tensor("dnt", [NE, 128, HHN, C], F32, kind="ExternalInput").ap()
    drt = nc.dram_tensor("rt", [C, 8], F32, kind="ExternalInput").ap()
    dwm = nc.dram_tensor("wm", [128, 12, CC, 512], F32, kind="ExternalInput").ap()
    dcb = nc.dram_tensor("cb", [128, 8], F32, kind="ExternalInput").ap()
    dout = nc.dram_tensor("out", [C, OWNT], F32, kind="ExternalOutput").ap()
    dscr = nc.dram_tensor("scr_mod", [6 * C], F32, kind="Internal").ap()
    dmask = nc.dram_tensor("scr_mask", [8, OWNT], F32, kind="Internal").ap()

    dbg = {}
    if stage == "h1":
        dbg["h1"] = nc.dram_tensor("dbg_h1", [128, CC, TL - PAD], BF16, kind="ExternalOutput").ap()
        dbg["h1p"] = nc.dram_tensor("dbg_h1p", [128, CC, PAD], BF16, kind="ExternalOutput").ap()
    elif stage == "mod":
        dbg["modv"] = nc.dram_tensor("dbg_modv", [128, 48], F32, kind="ExternalOutput").ap()
        dbg["modd"] = nc.dram_tensor("dbg_modd", [128, 72], F32, kind="ExternalOutput").ap()
        dbg["eqb"] = nc.dram_tensor("dbg_eqb", [128, 8], F32, kind="ExternalOutput").ap()
        dbg["mtm"] = nc.dram_tensor("dbg_mtm", [128, 8], F32, kind="ExternalOutput").ap()
    elif stage == "attn":
        dbg["yall"] = nc.dram_tensor("dbg_yall", [128, CC, OWNT], BF16, kind="ExternalOutput").ap()
    elif stage == "fdbg":
        dbg["dmask"] = nc.dram_tensor("dbg_dmask", [8, OWNT], F32, kind="ExternalOutput").ap()
        dbg["lg"] = nc.dram_tensor("dbg_lg", [128, 4, 32], F32, kind="ExternalOutput").ap()
    elif stage == "x1":
        dbg["x1"] = nc.dram_tensor("dbg_x1", [128, CC, OWNT], F32, kind="ExternalOutput").ap()
        dbg["h2"] = nc.dram_tensor("dbg_h2", [128, CC, OWNT], BF16, kind="ExternalOutput").ap()
        dbg["mrows"] = nc.dram_tensor("dbg_mrows", [8, OWNT], BF16, kind="ExternalOutput").ap()

    es = contextlib.ExitStack()
    with es:
        E = es.enter_context

        sems = {}
        for nm in ("sy", "gd", "gc", "pe", "ac", "ve", "mr", "m1", "dk",
                   "u0", "u1", "u2", "u3", "dn", "pw0", "pw1", "xa", "xb", "xq"):
            sems[nm] = [E(nc.semaphore("s_" + nm)), 0]

        def inc(which, instr, k=1):
            instr.then_inc(sems[which][0], k)
            sems[which][1] += k
            return sems[which][1]

        def cnt(which):
            return sems[which][1]

        def wait(eng, which, n=None):
            n = sems[which][1] if n is None else n
            if n > 0:
                eng.wait_ge(sems[which][0], n)

        TE, VE, SE, GE, SY = nc.tensor, nc.vector, nc.scalar, nc.gpsimd, nc.sync

        h2t = E(nc.sbuf_tensor("h2t", [128, CC, PAD], BF16))
        modv = E(nc.sbuf_tensor("modv", [128, 48], F32))
        modd = E(nc.sbuf_tensor("modd", [128, 72], F32))

        silu_c = E(nc.sbuf_tensor("silu_c", [128, 8], F32))
        cbs = E(nc.sbuf_tensor("cbs", [128, 8], F32))
        qbs = E(nc.sbuf_tensor("qbs", [128, 8], F32))
        qbs2 = E(nc.sbuf_tensor("qbs2", [128, 8], F32))
        obs = E(nc.sbuf_tensor("obs", [128, 8], F32))
        rts = E(nc.sbuf_tensor("rts", [128, CC, 8], F32))
        bvr_b = E(nc.sbuf_tensor("bvr_b", [1, C], BF16))
        ones_row = E(nc.sbuf_tensor("ones_row", [1, 512], F32))
        ones_row_b = E(nc.sbuf_tensor("ones_row_b", [1, 512], BF16))
        ones_col = E(nc.sbuf_tensor("ones_col", [128, 1], F32))
        eps1 = E(nc.sbuf_tensor("eps1", [1, 1], F32))
        iota_p = E(nc.sbuf_tensor("iota_p", [128, 1], F32))
        idn32 = E(nc.sbuf_tensor("idn32", [128, 128], F32))
        r48 = E(nc.sbuf_tensor("r48", [48, 128], F32))
        bv_off = []
        bv_tot = 0
        for h in range(NH):
            bv_off.append(bv_tot)
            bv_tot += max(WT[h], 3)
        GOFF = [0, 128, 384, 768]
        st_mean = E(nc.sbuf_tensor("st_mean", [1, 512], F32))
        st_msq = E(nc.sbuf_tensor("st_msq", [1, 512], F32))
        st_rstd = E(nc.sbuf_tensor("st_rstd", [1, 512], F32))
        sgbuf = E(nc.sbuf_tensor("sgbuf", [128, 2, 512], BF16))
        tmpf = E(nc.sbuf_tensor("tmpf", [128, 2, 512], F32))
        eqb = E(nc.sbuf_tensor("eqb", [128, 8], F32))
        remb = E(nc.sbuf_tensor("remb", [128, 1], F32))
        rmaxb = E(nc.sbuf_tensor("rmaxb", [128, 1], F32))
        mtm = E(nc.sbuf_tensor("mtm", [128, 8], F32))
        z8 = E(nc.sbuf_tensor("z8", [128, 8], F32))

        PB = [E(nc.psum_tensor(f"pb{i}", [128, 512], F32)) for i in range(8)]
        bank_free = {i: None for i in range(8)}

        def te_claim(b):
            if bank_free[b] is not None:
                which, n = bank_free[b]
                if n > 0:
                    TE.wait_ge(sems[which][0], n)
                bank_free[b] = None

        def release(b, which):
            bank_free[b] = (which, cnt(which))

        guard = {}

        def gwait(eng, key):
            if key in guard:
                which, n = guard[key]
                if n > 0:
                    eng.wait_ge(sems[which][0], n)
                del guard[key]

        def gset(key, which):
            guard[key] = (which, cnt(which))

        def vsync(instr):
            inc("ve", instr, 1)
            wait(VE, "ve")
            return instr

        CA1, NCA1, NSACA1, CB1, NCB1, NSBCB1, GA, GB, CGA = 0, 8, 16, 24, 32, 40, 48, 56, 64

        es_rb = contextlib.ExitStack()
        Erb = es_rb.enter_context
        ciota = Erb(nc.sbuf_tensor("ciota", [128, 512], F32, side="right"))
        Dm = [Erb(nc.sbuf_tensor(f"Dm_{c4}", [128, 128 * (c4 + 1)], BF16, side="right")) for c4 in range(4)]
        bv_all = Erb(nc.sbuf_tensor("bv_all", [128, bv_tot], F32, side="right"))
        valid_s = Erb(nc.sbuf_tensor("valid_s", [128, 48], F32, side="right"))
        es_rb2 = contextlib.ExitStack()
        x2buf = es_rb2.enter_context(nc.sbuf_tensor("x2buf", [128, 2, 512], F32, side="right"))
        tbuf = es_rb2.enter_context(nc.sbuf_tensor("tbuf", [128, 2, 512], F32, side="right"))

        inc("sy", SY.dma_start(out=valid_s[:], in_=dvalid), 16)
        inc("sy", SY.dma_start(out=cbs[:], in_=dcb), 16)
        inc("sy", SY.dma_start(out=qbs[:], in_=dqb), 16)
        inc("sy", SY.dma_start(out=obs[:], in_=dob), 16)
        inc("sy", SY.dma_start(out=rts[:], in_=drt.rearrange("(c p) e -> p c e", c=CC, p=128)), 16)
        inc("gd", GE.dma_start(out=bvr_b[:], in_=dbvr), 16)
        n_small_loads = cnt("sy")

        GE.memset(z8[:], 0.0)
        GE.memset(ones_row[:], 1.0)
        GE.memset(ones_row_b[:], 1.0)
        GE.memset(ones_col[:], 1.0)
        GE.memset(eps1[:], 1e-5)
        GE.iota(iota_p[:], pattern=[[0, 1]], base=0, channel_multiplier=1,
                allow_small_or_imprecise_dtypes=True)
        GE.iota(ciota[:], pattern=[[1, 512]], base=0, channel_multiplier=0,
                allow_small_or_imprecise_dtypes=True)
        GE.memset(idn32[:], 1.0)
        GE.affine_select(idn32[:], idn32[:], pattern=[[-1, 128]], compare_op=ALU.is_equal,
                         fill=0.0, channel_multiplier=1)
        for c4 in range(4):
            GE.iota(Dm[c4][:], pattern=[[128, c4 + 1], [-1, 128]], base=-128 * c4,
                    channel_multiplier=1, allow_small_or_imprecise_dtypes=True)
            ins = GE.affine_select(Dm[c4][:], Dm[c4][:], pattern=[[-128, c4 + 1], [1, 128]],
                                   base=128 * c4, compare_op=ALU.is_ge, fill=NEG,
                                   channel_multiplier=-1)
        inc("gc", ins, 1)

        wait(VE, "sy", n_small_loads)
        wait(VE, "gc")
        VE.tensor_scalar(qbs2[:], qbs[:], 0.125, 0.0, ALU.mult, ALU.add)
        for h in range(NH):
            for d in range(1, max(WT[h], 3) + 1):
                VE.tensor_scalar(bv_all[:, bv_off[h] + d - 1: bv_off[h] + d], iota_p[:],
                                 SLOPES[h], -128.0 * d * SLOPES[h], ALU.mult, ALU.add)
        wait(SE, "sy", n_small_loads)
        inc("ac", SE.activation(out=silu_c[:], in_=cbs[:], func=AF.Silu), 1)

        es_r3 = contextlib.ExitStack()
        XBUF = [es_r3.enter_context(nc.sbuf_tensor(f"xbuf{i}", [128, CC, 512], F32, side="right")) for i in range(2)]

        for j in range(12):
            b = j % 2
            xsem = "xa" if b == 0 else "xb"
            gwait(SY, f"xbuf{b}")
            inc(xsem, SY.dma_start(out=XBUF[b][:], in_=dwm[:, j]), 16)
            wait(TE, xsem)
            wait(TE, "ac")
            te_claim(4)
            for cc in range(CC):
                ins = TE.matmul(PB[4][0:1, :], silu_c[:, cc:cc + 1], XBUF[b][:, cc, :],
                                start=(cc == 0), stop=(cc == CC - 1))
            inc("pe", ins, 1)
            gset(f"xbuf{b}", "pe")
            wait(SE, "pe")
            gwait(SE, "modrow1")
            inc("ac", SE.copy(out=st_mean[:], in_=PB[4][0:1, :]), 1)
            release(4, "ac")
            wait(GE, "ac")
            inc("gd", GE.dma_start(out=dscr[512 * j: 512 * (j + 1)].rearrange("(o a) -> o a", o=1),
                                   in_=st_mean[:]), 16)
            gset("modrow1", "gd")
        wait(GE, "gd")
        inc("gd", GE.dma_start(out=r48[:], in_=dscr.rearrange("(a b) -> a b", a=48, b=128)), 16)
        wait(TE, "gd")
        wait(TE, "gc")
        te_claim(5)
        inc("pe", TE.transpose(PB[5][:, 0:48], r48[:], idn32[0:48, 0:48]), 1)
        wait(SE, "pe")
        inc("ac", SE.copy(out=modv[:], in_=PB[5][:, 0:48]), 1)
        release(5, "ac")

        wait(VE, "ac")
        wait(VE, "sy", n_small_loads)
        vsync(VE.tensor_scalar(modd[:, CA1:CA1 + 8], modv[:, 8:16], 1.0, 1.0, ALU.mult, ALU.add))
        vsync(VE.tensor_scalar(modd[:, NCA1:NCA1 + 8], modd[:, CA1:CA1 + 8], -1.0, 0.0, ALU.mult, ALU.add))
        vsync(VE.tensor_scalar(modd[:, CB1:CB1 + 8], modv[:, 32:40], 1.0, 1.0, ALU.mult, ALU.add))
        vsync(VE.tensor_scalar(modd[:, NCB1:NCB1 + 8], modd[:, CB1:CB1 + 8], -1.0, 0.0, ALU.mult, ALU.add))
        for (src, dst) in ((16, GA), (40, GB)):
            vsync(VE.tensor_scalar(modd[:, dst:dst + 8], modv[:, src:src + 8], -1.0, 0.0, ALU.mult, ALU.add))
            vsync(VE.tensor_tensor(modd[:, dst:dst + 8], modd[:, dst:dst + 8], modv[:, src:src + 8], ALU.max))
            vsync(VE.tensor_scalar(modd[:, dst:dst + 8], modd[:, dst:dst + 8], 0.25, 0.0, ALU.max, ALU.add))
            vsync(VE.tensor_tensor(eqb[:], modv[:, src:src + 8], z8[:], ALU.is_ge))
            vsync(VE.tensor_scalar(mtm[:], eqb[:], 2.0, -1.0, ALU.mult, ALU.add))
            vsync(VE.tensor_tensor(modd[:, dst:dst + 8], modd[:, dst:dst + 8], mtm[:], ALU.mult))
        vsync(VE.tensor_tensor(modd[:, CGA:CGA + 8], modd[:, GA:GA + 8], obs[:], ALU.mult))

        if stage == "mod":
            wait(SY, "ve")
            inc("sy", SY.dma_start(out=dbg["modv"], in_=modv[:]), 16)
            inc("sy", SY.dma_start(out=dbg["modd"], in_=modd[:]), 16)
            inc("sy", SY.dma_start(out=dbg["eqb"], in_=eqb[:]), 16)
            inc("sy", SY.dma_start(out=dbg["mtm"], in_=mtm[:]), 16)
            wait(GE, "sy")
            es_r3.close()
            return nc

        def emit_ln_h(src_tile, dst_fn, ca1_col, sh_col, h2f32_dst=None, post_cc=None):
            te_claim(0)
            te_claim(1)
            for cc in range(CC):
                b = cc % 2
                gwait(SE, f"x2_{b}")
                inc("ac", SE.activation(out=x2buf[:, b, :], in_=src_tile(cc), func=AF.Square), 1)
                ins = TE.matmul(PB[0][0:1, :], ones_col[:], src_tile(cc),
                                start=(cc == 0), stop=(cc == CC - 1))
                if cc == CC - 1:
                    inc("pe", ins, 1)
                wait(TE, "ac")
                ins = TE.matmul(PB[1][0:1, :], ones_col[:], x2buf[:, b, :],
                                start=(cc == 0), stop=(cc == CC - 1))
                inc("pe", ins, 1)
                gset(f"x2_{b}", "pe")
            wait(VE, "pe")
            vsync(VE.tensor_scalar(st_mean[:], PB[0][0:1, :], 1.0 / C, 0.0, ALU.mult, ALU.add))
            vsync(VE.tensor_tensor(st_msq[:], st_mean[:], st_mean[:], ALU.mult))
            ins = VE.scalar_tensor_tensor(st_msq[:], PB[1][0:1, :], 1.0 / C, st_msq[:],
                                          ALU.mult, ALU.subtract)
            inc("ve", ins, 1)
            release(0, "ve")
            release(1, "ve")
            wait(SE, "ve")
            SE.activation(out=st_msq[:], in_=st_msq[:], func=AF.Ln, bias=eps1[:])
            ins = SE.activation(out=st_rstd[:], in_=st_msq[:], func=AF.Exp, scale=-0.5)
            inc("ac", ins, 1)
            wait(TE, "ac")
            te_claim(2)
            te_claim(3)
            inc("pe", TE.matmul(PB[2][:], ones_row[:, 0:128], st_rstd[:], start=True, stop=True), 1)
            inc("pe", TE.matmul(PB[3][:], ones_row[:, 0:128], st_mean[:], start=True, stop=True), 1)
            for cc in range(CC):
                wait(VE, "pe")
                b = cc % 2
                VE.tensor_tensor(tbuf[:, b, :], src_tile(cc), PB[3][:], ALU.subtract)
                VE.scalar_tensor_tensor(tbuf[:, 1 - b, :], tbuf[:, b, :],
                                        modd[:, ca1_col + cc:ca1_col + cc + 1],
                                        PB[2][:], ALU.mult, ALU.mult)
                if h2f32_dst is not None:
                    VE.tensor_scalar(h2f32_dst(cc), tbuf[:, 1 - b, :],
                                     modv[:, sh_col + cc:sh_col + cc + 1], 0.0, ALU.add, ALU.add)
                    ins = VE.tensor_copy(dst_fn(cc), h2f32_dst(cc))
                    inc("ve", ins, 1)
                else:
                    ins = VE.tensor_scalar(dst_fn(cc), tbuf[:, 1 - b, :],
                                           modv[:, sh_col + cc:sh_col + cc + 1], 0.0, ALU.add, ALU.add)
                    inc("ve", ins, 1)
                if post_cc is not None:
                    post_cc(cc)
            release(2, "ve")
            release(3, "ve")

        es_r2 = contextlib.ExitStack()
        h1r = es_r2.enter_context(nc.sbuf_tensor("h1r", [128, CC, TL - PAD], BF16))

        def h1_slice(cc, u0, w):
            if u0 < PAD:
                return h2t[:, cc, u0:u0 + w]
            return h1r[:, cc, u0 - PAD:u0 - PAD + w]

        for tt in range(TT_N):
            b = tt % 2
            u0 = tt * 512
            xsem = "xa" if b == 0 else "xb"
            gwait(SY, f"xbuf{b}")
            inc(xsem, SY.dma_start(out=XBUF[b][:], in_=dx[:, tt]), 16)
            wait(TE, xsem)
            wait(SE, xsem)
            wait(VE, xsem)

            def src_tile(cc, _b=b):
                return XBUF[_b][:, cc, :]

            def dst_fn(cc, _u0=u0):
                return h1_slice(cc, _u0, 512)

            emit_ln_h(src_tile, dst_fn, CA1, 0)
            gset(f"xbuf{b}", "ve")

        if stage == "h1":
            wait(SY, "ve")
            inc("sy", SY.dma_start(out=dbg["h1"], in_=h1r[:]), 16)
            inc("sy", SY.dma_start(out=dbg["h1p"], in_=h2t[:]), 16)
            wait(GE, "sy")
            es_r3.close(); es_r2.close()
            return nc

        es_r3.close()
        es_rb2.close()
        es_r4 = contextlib.ExitStack()
        y_all = es_r4.enter_context(nc.sbuf_tensor("y_all", [128, CC, OWNT], BF16, side="right"))
        k2 = es_r3.enter_context(nc.sbuf_tensor("k2", [128, TT_N * 512], BF16, side="right"))
        q2 = es_r3.enter_context(nc.sbuf_tensor("q2", [128, OWN, 512], BF16, side="right"))
        v2 = es_r3.enter_context(nc.sbuf_tensor("v2", [128, KT_MAX, 130], BF16, side="right"))
        Gcur = es_r3.enter_context(nc.sbuf_tensor("Gcur", [128, 1280], BF16, side="right"))
        colf = es_r3.enter_context(nc.sbuf_tensor("colf", [128, 2, 512], BF16, side="right"))
        qwt = es_r3.enter_context(nc.sbuf_tensor("qwt", [128, 2, CC, 128], BF16, side="right"))
        kvwt = es_r3.enter_context(nc.sbuf_tensor("kvwt", [128, 2, CC, 256], BF16, side="right"))
        e1buf = es_r3.enter_context(nc.sbuf_tensor("e1buf", [128, 3, 512], BF16, side="right"))
        pbuf = es_r3.enter_context(nc.sbuf_tensor("pbuf", [128, 3, 512], BF16, side="right"))
        recip_rb = es_r3.enter_context(nc.sbuf_tensor("recip_rb", [1, 512], BF16, side="right"))
        srow = st_mean
        recip_r = st_msq
        srow_b = sgbuf[:, 1, :]
        recips = sgbuf[:, 0, :]

        wait(GE, "ve")
        wait(GE, "pe")
        inc("gc", GE.memset(v2[:], 1.0), 1)

        norm_pending = [None]
        yb_idx = [0]

        for hp in range(HPAIRS):
            if norm_pending[0] is not None:
                norm_pending[0][0]()
                norm_pending[0][1]()
                norm_pending[0] = None
            wslot = hp % 2
            psem = "pw0" if wslot == 0 else "pw1"
            if hp == 0:
                gwait(GE, f"pairw_{wslot}")
                inc(psem, GE.dma_start(out=qwt[:, wslot], in_=dqw[hp]), 16)
                inc(psem, GE.dma_start(out=kvwt[:, wslot], in_=dkvw[hp]), 16)
            wait(TE, psem)
            wait(TE, "gc")

            for tt in range(TT_N):
                u0 = tt * 512
                kb = 4 + tt % 2
                te_claim(kb)
                for cc in range(CC):
                    ins = TE.matmul(PB[kb][:], kvwt[:, wslot, cc, 0:128], h1_slice(cc, u0, 512),
                                    start=(cc == 0), stop=(cc == CC - 1))
                inc("pe", ins, 1)
                wait(SE, "pe")
                inc("ac", SE.copy(out=k2[:, u0:u0 + 512], in_=PB[kb][:]), 1)
                release(kb, "ac")
            for qi in range(OWN):
                off = qi * 1024
                kb = 4 + qi % 2
                te_claim(kb)
                for cc in range(CC):
                    ins = TE.matmul(PB[kb][:], qwt[:, wslot, cc, :], h1r[:, cc, off:off + 512],
                                    start=(cc == 0), stop=(cc == CC - 1))
                inc("pe", ins, 1)
                wait(SE, "pe")
                inc("ac", SE.activation(out=q2[:, qi, :], in_=PB[kb][:], func=AF.Identity,
                                        scale=0.125, bias=qbs2[:, hp:hp + 1]), 1)
                release(kb, "ac")
            for kt in range(KT_MAX):
                u0 = kt * 128
                vb = 3 if kt % 2 == 0 else 6
                te_claim(vb)
                for cc in range(CC):
                    ins = TE.matmul(PB[vb][:, 0:128], h1_slice(cc, u0, 128), kvwt[:, wslot, cc, 128:256],
                                    start=(cc == 0), stop=(cc == CC - 1))
                inc("pe", ins, 1)
                wait(VE, "pe")
                VE.tensor_copy(v2[:, kt, 0:64], PB[vb][:, 0:64])
                ins = VE.tensor_copy(v2[:, kt, 65:129], PB[vb][:, 64:128])
                inc("ve", ins, 1)
                release(vb, "ve")
            if hp + 1 < HPAIRS:
                ns = (hp + 1) % 2
                nsem = "pw0" if ns == 0 else "pw1"
                gwait(GE, f"pairw_{ns}")
                inc(nsem, GE.dma_start(out=qwt[:, ns], in_=dqw[hp + 1]), 16)
                inc(nsem, GE.dma_start(out=kvwt[:, ns], in_=dkvw[hp + 1]), 16)
            gset(f"pairw_{wslot}", "pe")

            wait(SE, "ve")
            for hip in range(2):
                ins = SE.activation(out=colf[:, hip, :], in_=ciota[:], func=AF.Exp,
                                    scale=-SLOPES[2 * hp + hip])
            inc("ac", ins, 1)
            wait(TE, "ac")

            for hip in range(2):
                h = 2 * hp + hip
                r0 = 64 * hip
                vlo, vhi = (0, 65) if hip == 0 else (65, 130)
                gwait(SE, "gcur")
                for c4 in range(4):
                    ins = SE.activation(out=Gcur[:, GOFF[c4]:GOFF[c4] + 128 * (c4 + 1)],
                                        in_=Dm[c4][:], func=AF.Exp, scale=SLOPES[h])
                inc("ac", ins, 1)
                for qi in range(OWN):
                    q0t = (PAD + qi * 1024) // 128
                    yb = 2 + (yb_idx[0] % 2)
                    yb_idx[0] += 1
                    slots = [("off", q0t - dlt, dlt) for dlt in range(WT[h], 0, -1)]
                    slots += [("diag", c4) for c4 in range(4)]
                    n_s = len(slots)
                    av_ready = {}

                    SBMAP = (0, 1, 4)

                    def emit_front(i):
                        kind = slots[i][0]
                        r = i % 3
                        sb = SBMAP[r]
                        te_claim(sb)
                        if kind == "off":
                            _, kt, dlt = slots[i]
                            s_rdy = inc("pe", TE.matmul(PB[sb][:], k2[r0:r0 + 64, kt * 128:kt * 128 + 128],
                                                        q2[r0:r0 + 64, qi, :], start=True, stop=True), 1)
                            wait(SE, "pe", s_rdy)
                            gwait(SE, f"e1_{r}")
                            inc("ac", SE.activation(out=e1buf[:, r, :], in_=PB[sb][:], func=AF.Exp,
                                                    bias=bv_all[:, bv_off[h] + dlt - 1:bv_off[h] + dlt]), 1)
                            release(sb, "ac")
                            gwait(VE, f"pb_{r}")
                            wait(VE, "ac")
                            ins = VE.scalar_tensor_tensor(pbuf[:, r, :], e1buf[:, r, :],
                                                          valid_s[:, kt:kt + 1], colf[:, hip, :],
                                                          ALU.mult, ALU.mult)
                            inc("ve", ins, 1)
                            gset(f"e1_{r}", "ve")
                        else:
                            _, c4 = slots[i]
                            w = 128 * (c4 + 1)
                            for j4 in range(c4 + 1):
                                kt = q0t + j4
                                ins = TE.matmul(PB[sb][:, 128 * j4:128 * j4 + 128],
                                                k2[r0:r0 + 64, kt * 128:kt * 128 + 128],
                                                q2[r0:r0 + 64, qi, 128 * c4:128 * c4 + 128],
                                                start=True, stop=True)
                            s_rdy = inc("pe", ins, 1)
                            wait(SE, "pe", s_rdy)
                            gwait(SE, f"e1_{r}")
                            inc("ac", SE.activation(out=e1buf[:, r, 0:w], in_=PB[sb][:, 0:w],
                                                    func=AF.Exp), 1)
                            release(sb, "ac")
                            gwait(VE, f"pb_{r}")
                            wait(VE, "ac")
                            ins = VE.tensor_tensor(pbuf[:, r, 0:w], e1buf[:, r, 0:w],
                                                   Gcur[:, GOFF[c4]:GOFF[c4] + w], ALU.mult)
                            inc("ve", ins, 1)
                            gset(f"e1_{r}", "ve")
                        av_ready[i] = cnt("ve")

                    def emit_av(i, _yb=yb, _q0t=q0t):
                        kind = slots[i][0]
                        r = i % 3
                        wait(TE, "ve", av_ready[i])
                        if kind == "off":
                            _, kt, dlt = slots[i]
                            inc("pe", TE.matmul(PB[_yb][0:65, :], v2[:, kt, vlo:vhi],
                                                pbuf[:, r, :], start=(i == 0), stop=False), 1)
                        else:
                            _, c4 = slots[i]
                            for j4 in range(c4 + 1):
                                kt = _q0t + j4
                                ins = TE.matmul(PB[_yb][0:65, 128 * c4:128 * c4 + 128],
                                                v2[:, kt, vlo:vhi],
                                                pbuf[:, r, 128 * j4:128 * j4 + 128],
                                                start=False, stop=False)
                            inc("pe", ins, 1)
                        gset(f"pb_{r}", "pe")

                    te_claim(yb)
                    for i in range(n_s + 1):
                        if i < n_s:
                            emit_front(i)
                        if i == 1 and norm_pending[0] is not None:
                            norm_pending[0][0]()
                        if i == min(5, n_s) and norm_pending[0] is not None:
                            norm_pending[0][1]()
                            norm_pending[0] = None
                        if i >= 1:
                            emit_av(i - 1)
                    av_last = cnt("pe")

                    def make_norm(_yb=yb, _h=h, _qi=qi, _av=None):
                        st = {}

                        def norm_dve():
                            wait(VE, "pe", _av)
                            vsync(VE.tensor_copy(srow[:], PB[_yb][64:65, :]))
                            VE.tensor_copy(srow_b[0:1, :], srow[:])
                            vsync(VE.reciprocal(recip_r[:], srow[:]))
                            ins = VE.tensor_copy(recip_rb[:], recip_r[:])
                            inc("ve", ins, 1)
                            st["dve"] = cnt("ve")

                        def norm_te():
                            wait(TE, "ve", st["dve"])
                            inc("pe", TE.matmul(PB[_yb][0:64, :], bvr_b[:, 64 * _h:64 * _h + 64],
                                                srow_b[0:1, :], start=False, stop=True), 1)
                            te_claim(7)
                            inc("pe", TE.matmul(PB[7][0:64, :], ones_row_b[:, 0:64], recip_rb[:],
                                                start=True, stop=True), 1)
                            wait(VE, "pe")
                            VE.tensor_copy(recips[0:64, :], PB[7][0:64, :])
                            ccv = _h // 2
                            rr = 64 * (_h % 2)
                            ins = VE.tensor_tensor(y_all[rr:rr + 64, ccv, _qi * 512:_qi * 512 + 512],
                                                   PB[_yb][0:64, :], recips[0:64, :], ALU.mult)
                            inc("ve", ins, 1)
                            release(_yb, "ve")
                            release(7, "ve")
                        return (norm_dve, norm_te)

                    norm_pending[0] = make_norm(_av=av_last)
                gset("gcur", "ve")
        if norm_pending[0] is not None:
            norm_pending[0][0]()
            norm_pending[0][1]()
            norm_pending[0] = None

        if stage == "attn":
            wait(SY, "ve")
            inc("sy", SY.dma_start(out=dbg["yall"], in_=y_all[:]), 16)
            wait(GE, "sy")
            es_r3.close(); es_r4.close(); es_r2.close()
            return nc

        es_r3.close()
        x2buf = es_r3.enter_context(nc.sbuf_tensor("x2buf_p3", [128, 2, 512], F32, side="right"))
        tbuf = es_r3.enter_context(nc.sbuf_tensor("tbuf_p3", [128, 2, 512], F32, side="right"))
        owt = es_r3.enter_context(nc.sbuf_tensor("owt", [128, CC, C], BF16, side="right"))
        xq2 = es_r3.enter_context(nc.sbuf_tensor("xq2", [128, 2, 512], F32, side="right"))
        h2f_c = es_r3.enter_context(nc.sbuf_tensor("h2f_c", [128, 512], F32, side="right"))
        logits_sb = es_r3.enter_context(nc.sbuf_tensor("logits_sb", [128, 32], F32, side="right"))
        es_r2.close()
        yacc = es_r2.enter_context(nc.sbuf_tensor("yacc", [128, CC, OWNT], F32))
        m8buf = es_r2.enter_context(nc.sbuf_tensor("m8buf", [8, 128], F32))
        wait(GE, "pe")
        wait(GE, "ve")
        inc("gd", GE.dma_start(out=owt[:], in_=dow), 16)

        lgdump = None
        if stage == "fdbg":
            lgdump = es_r2.enter_context(nc.sbuf_tensor("lgdump", [128, 4, 32], F32))
        first_p3 = True
        for qi in range(OWN):
            u0 = PAD + qi * 1024
            if first_p3:
                wait(SY, "pe")
                wait(SY, "ve")
                wait(TE, "ve")
                first_p3 = False
            for cco in range(CC):
                ob_ = cco % 2
                gwait(SY, f"xq_{ob_}")
                inc("xq", SY.dma_start(out=xq2[:, ob_, :], in_=dx[:, u0 // 512, cco]), 16)
                wait(TE, "gd")
                te_claim(6 + ob_)
                for cci in range(CC):
                    ins = TE.matmul(PB[6 + ob_][:], owt[:, cci, 128 * cco:128 * cco + 128],
                                    y_all[:, cci, qi * 512:qi * 512 + 512],
                                    start=(cci == 0), stop=(cci == CC - 1))
                inc("pe", ins, 1)
                wait(VE, "pe")
                wait(VE, "xq")
                VE.scalar_tensor_tensor(tmpf[:, ob_, :], PB[6 + ob_][:], modd[:, GA + cco:GA + cco + 1],
                                        xq2[:, ob_, :], ALU.mult, ALU.add)
                ins = VE.tensor_scalar(yacc[:, cco, qi * 512:qi * 512 + 512], tmpf[:, ob_, :],
                                       modd[:, CGA + cco:CGA + cco + 1], 0.0, ALU.add, ALU.add)
                inc("ve", ins, 1)
                release(6 + ob_, "ve")
                gset(f"xq_{ob_}", "ve")

            def src2(cc, _qi=qi):
                return yacc[:, cc, _qi * 512:_qi * 512 + 512]

            def dst2(cc, _qi=qi):
                return h2t[:, cc, _qi * 512:_qi * 512 + 512]

            def dsth2f(cc):
                return h2f_c[:]

            def post_cc(cc):
                te_claim(5)
                wait(TE, "ve")
                for sub in range(4):
                    ins = TE.matmul(PB[5][:, 8 * sub:8 * sub + 8],
                                    h2f_c[:, 128 * sub:128 * sub + 128], rts[:, cc, :],
                                    start=True, stop=True)
                inc("pe", ins, 1)
                wait(VE, "pe")
                if cc == 0:
                    vsync(VE.tensor_copy(logits_sb[:], PB[5][:, 0:32]))
                else:
                    vsync(VE.tensor_tensor(logits_sb[:], logits_sb[:], PB[5][:, 0:32], ALU.add))
                release(5, "ve")

            wait(TE, "ve")
            wait(SE, "ve")
            emit_ln_h(src2, dst2, CB1, 24, h2f32_dst=dsth2f, post_cc=post_cc)

            if stage == "fdbg":
                ins = VE.tensor_copy(lgdump[:, qi, :], logits_sb[:])
                inc("ve", ins, 1)
            for sub in range(4):
                wait(VE, "pe")
                vsync(VE.reduce_max(rmaxb[:], logits_sb[:, 8 * sub:8 * sub + 8], axis=AXX))
                vsync(VE.tensor_scalar(mtm[:], logits_sb[:, 8 * sub:8 * sub + 8], 0.0, rmaxb[:],
                                       ALU.mult, ALU.add))
                vsync(VE.tensor_tensor(eqb[:], logits_sb[:, 8 * sub:8 * sub + 8], mtm[:], ALU.is_ge))
                vsync(VE.memset(remb[:], 1.0))
                for e_ in range(8):
                    vsync(VE.tensor_tensor(mtm[:, e_:e_ + 1], eqb[:, e_:e_ + 1], remb[:], ALU.mult))
                    vsync(VE.scalar_tensor_tensor(remb[:], mtm[:, e_:e_ + 1], -1.0, remb[:],
                                                  ALU.mult, ALU.add))
                wait(TE, "ve")
                te_claim(4)
                inc("pe", TE.transpose(PB[4][0:8, 0:128], mtm[:], idn32[:]), 1)
                wait(VE, "pe")
                gwait(VE, "m8")
                ins = VE.tensor_copy(m8buf[:], PB[4][0:8, 0:128])
                inc("ve", ins, 1)
                release(4, "ve")
                wait(SY, "ve")
                inc("dk", SY.dma_start(out=dmask[:, qi * 512 + 128 * sub: qi * 512 + 128 * sub + 128],
                                       in_=m8buf[:]), 16)
                gset("m8", "dk")
            release(5, "ve")

        if stage == "x1":
            wait(SY, "ve")
            inc("sy", SY.dma_start(out=dbg["x1"], in_=yacc[:]), 16)
            inc("sy", SY.dma_start(out=dbg["h2"], in_=h2t[:]), 16)
            wait(GE, "sy")
            mrall = es_r3.enter_context(nc.sbuf_tensor("mrall", [8, OWNT], BF16, side="right"))
            inc("gd", GE.dma_start(out=mrall[:], in_=dmask), 16)
            wait(SY, "gd")
            inc("sy", SY.dma_start(out=dbg["mrows"], in_=mrall[:]), 16)
            wait(GE, "sy")
            es_r3.close(); es_r4.close(); es_r2.close()
            return nc

        if stage == "fdbg":
            wait(SY, "ve")
            inc("sy", SY.dma_start(out=dbg["lg"], in_=lgdump[:]), 16)
            wait(GE, "sy")
            mrall2 = es_r3.enter_context(nc.sbuf_tensor("mrall2", [8, OWNT], F32, side="right"))
            inc("gd", GE.dma_start(out=mrall2[:], in_=dmask), 16)
            wait(SY, "gd")
            inc("sy", SY.dma_start(out=dbg["dmask"], in_=mrall2[:]), 16)
            wait(GE, "sy")
            es_r3.close(); es_r4.close(); es_rb.close(); es_r2.close()
            return nc

        es_r3.close()
        es_r4.close()
        es_rb.close()
        dn_s = es_r3.enter_context(nc.sbuf_tensor("dn_s", [128, HHN, C], BF16, side="right"))
        act_b = es_r3.enter_context(nc.sbuf_tensor("act_b", [128, HHN, 1024], BF16, side="right"))
        ugb = es_r3.enter_context(nc.sbuf_tensor("ugb", [128, 4, 2, CC, 128], BF16, side="right"))
        maskb = es_r3.enter_context(nc.sbuf_tensor("maskb", [128, OWNT], BF16, side="right"))
        mrow1 = es_r3.enter_context(nc.sbuf_tensor("mrow1", [1, 2, 512], BF16, side="right"))

        wait(GE, "pe")
        wait(GE, "ve")
        wait(GE, "dk")

        def emit_mask(et):
            for sg in range(4):
                mb = sg % 2
                msem = "mr" if mb == 0 else "m1"
                gwait(GE, f"mrow_{mb}")
                inc(msem, GE.dma_start(out=mrow1[:, mb, :],
                                       in_=dmask[et:et + 1, 512 * sg:512 * sg + 512]), 16)
                wait(TE, msem)
                te_claim(6)
                inc("pe", TE.matmul(PB[6][:], ones_row_b[:, 0:128], mrow1[0:1, mb, :],
                                    start=True, stop=True), 1)
                gset(f"mrow_{mb}", "pe")
                wait(VE, "pe")
                ins = VE.tensor_copy(maskb[:, 512 * sg:512 * sg + 512], PB[6][:])
                inc("ve", ins, 1)
                release(6, "ve")

        emit_mask(0)
        ug_ready = {}
        dn_guard = 0
        for et in range(NE):
            dn_chunks = list(ddnt[et, :, 4 * ci:4 * ci + 4, :] for ci in range(4))
            for hfp in range(2):
                t0 = hfp * 1024
                for hh in range(HHN):
                    al = hh % 4
                    usem = ("u0", "u1", "u2", "u3")[al]
                    gwait(GE, f"ug_{al}")
                    inc(usem, GE.dma_start(out=ugb[:, al, 0, :, :], in_=dupt[et, hh]), 16)
                    inc(usem, GE.dma_start(out=ugb[:, al, 1, :, :], in_=dgat[et, hh]), 16)
                    if hfp == 0 and hh % 4 == 1:
                        ci = hh // 4
                        if dn_guard > 0:
                            wait(GE, "pe", dn_guard)
                        inc("dn", GE.dma_start(out=dn_s[:, 4 * ci:4 * ci + 4, :],
                                               in_=dn_chunks[ci]), 16)
                    for tf in range(2):
                        al2 = tf % 2
                        wait(TE, usem)
                        te_claim(0 + al2)
                        for cc in range(CC):
                            ins = TE.matmul(PB[0 + al2][:], ugb[:, al, 0, cc, :],
                                            h2t[:, cc, t0 + 512 * tf:t0 + 512 * tf + 512],
                                            start=(cc == 0), stop=(cc == CC - 1))
                        inc("pe", ins, 1)
                        te_claim(2 + al2)
                        for cc in range(CC):
                            ins = TE.matmul(PB[2 + al2][:], ugb[:, al, 1, cc, :],
                                            h2t[:, cc, t0 + 512 * tf:t0 + 512 * tf + 512],
                                            start=(cc == 0), stop=(cc == CC - 1))
                        inc("pe", ins, 1)
                        if tf == 1:
                            gset(f"ug_{al}", "pe")
                        wait(SE, "pe")
                        gwait(SE, f"sg_{al2}")
                        inc("ac", SE.activation(out=sgbuf[:, al2, :], in_=PB[2 + al2][:],
                                                func=AF.Silu), 1)
                        release(2 + al2, "ac")
                        wait(VE, "ac")
                        wait(VE, "pe")
                        ins = VE.tensor_tensor(act_b[:, hh, 512 * tf:512 * tf + 512],
                                               sgbuf[:, al2, :], PB[0 + al2][:], ALU.mult)
                        inc("ve", ins, 1)
                        gset(f"sg_{al2}", "ve")
                        release(0 + al2, "ve")
                if hfp == 0 and et + 1 < EXP + 1 and et + 1 < EXP:
                    pass
                wait(TE, "ve")
                wait(TE, "dn")
                for tf in range(2):
                    for cc in range(CC):
                        db = 4 + (cc + 2 * tf) % 2
                        te_claim(db)
                        for hh in range(HHN):
                            ins = TE.matmul(PB[db][:], dn_s[:, hh, 128 * cc:128 * cc + 128],
                                            act_b[:, hh, 512 * tf:512 * tf + 512],
                                            start=(hh == 0), stop=(hh == HHN - 1))
                        inc("pe", ins, 1)
                        wait(VE, "pe")
                        tt0 = t0 + 512 * tf
                        yslice = yacc[:, cc, tt0:tt0 + 512]
                        if et < EXP:
                            VE.tensor_tensor(tmpf[:, cc % 2, :], PB[db][:],
                                             maskb[:, tt0:tt0 + 512], ALU.mult)
                            ins = VE.scalar_tensor_tensor(yslice, tmpf[:, cc % 2, :],
                                                          modd[:, GB + cc:GB + cc + 1], yslice,
                                                          ALU.mult, ALU.add)
                        else:
                            ins = VE.scalar_tensor_tensor(yslice, PB[db][:],
                                                          modd[:, GB + cc:GB + cc + 1], yslice,
                                                          ALU.mult, ALU.add)
                        inc("ve", ins, 1)
                        release(db, "ve")
                if hfp == 1 and et + 1 < EXP:
                    emit_mask(et + 1)
            dn_guard = cnt("pe")

        wait(SY, "ve")
        inc("sy", SY.dma_start(out=dout.rearrange("(c p) t -> p c t", c=CC, p=128), in_=yacc[:]), 16)
        wait(GE, "sy")
        es_r3.close(); es_r2.close()
    return nc


def _stage_inputs(L0_attn_x, L0_attn_q_w, L0_attn_q_b, L0_attn_kv_w, L0_attn_kv_b,
                  L0_attn_o_w, L0_attn_o_b, L0_ffn_up, L0_ffn_gate, L0_ffn_down,
                  L0_ffn_router, L0_ffn_w_up_s, L0_ffn_w_gate_s, L0_ffn_w_down_s,
                  L0_c, L0_wmod):
    f = np.float32
    x = np.ascontiguousarray(np.asarray(L0_attn_x, dtype=f))
    q_w = np.asarray(L0_attn_q_w, f)
    q_b = np.asarray(L0_attn_q_b, f)
    kv_w = np.asarray(L0_attn_kv_w, f)
    kv_b = np.asarray(L0_attn_kv_b, f)
    o_w = np.asarray(L0_attn_o_w, f)
    ow_c = np.ascontiguousarray(o_w.reshape(CC, 128, C).transpose(1, 0, 2))
    o_b = np.asarray(L0_attn_o_b, f)
    up = np.asarray(L0_ffn_up, f)
    gate = np.asarray(L0_ffn_gate, f)
    down = np.asarray(L0_ffn_down, f)
    router = np.asarray(L0_ffn_router, f)
    wus = np.asarray(L0_ffn_w_up_s, f)
    wgs = np.asarray(L0_ffn_w_gate_s, f)
    wds = np.asarray(L0_ffn_w_down_s, f)
    c_ = np.asarray(L0_c, f)
    wm = np.asarray(L0_wmod, f)
    wm = np.ascontiguousarray(wm.reshape(CC, 128, 12, 512).transpose(1, 2, 0, 3))

    qw_blk = np.ascontiguousarray(q_w.reshape(CC, 128, HPAIRS, 128).transpose(2, 1, 0, 3))
    qb_col = np.ascontiguousarray(q_b.reshape(8, 128).T)
    kvw_blk = np.stack([np.concatenate([kv_w[:, 128 * hp:128 * hp + 128],
                                        kv_w[:, C + 128 * hp:C + 128 * hp + 128]], axis=1)
                        for hp in range(HPAIRS)])
    kvw_blk = np.ascontiguousarray(kvw_blk.reshape(HPAIRS, CC, 128, 256).transpose(0, 2, 1, 3))
    bvr = np.ascontiguousarray(kv_b[C:2 * C][None, :])
    ob_col = np.ascontiguousarray(o_b.reshape(8, 128).T)
    up_all = np.concatenate([up, wus[None]], axis=0)
    ga_all = np.concatenate([gate, wgs[None]], axis=0)
    dn_all = np.concatenate([down, wds[None]], axis=0)
    upt = np.ascontiguousarray(
        up_all.reshape(NE, HHN, 128, CC, 128).transpose(0, 1, 4, 3, 2))
    gat = np.ascontiguousarray(
        ga_all.reshape(NE, HHN, 128, CC, 128).transpose(0, 1, 4, 3, 2))
    dnt = np.ascontiguousarray(
        dn_all.transpose(0, 2, 1).reshape(NE, HHN, 128, C).transpose(0, 2, 1, 3))
    rt = np.ascontiguousarray(router.T)
    ow_c = np.ascontiguousarray(o_w.reshape(CC, 128, C).transpose(1, 0, 2))

    in_maps = []
    for core in range(8):
        b, par = core // 2, core % 2
        xs = np.zeros((C, TL), f)
        sh = PAD - 512 * par
        lo, hi = sh, min(TL, sh + TSEQ)
        xs[:, lo:hi] = x[b].T[:, 0:hi - lo]
        xs = np.ascontiguousarray(xs.reshape(CC, 128, TL // 512, 512).transpose(1, 2, 0, 3))
        valid = np.zeros((128, 48), f)
        uu = np.arange(48 * 128).reshape(48, 128).T
        g = uu - sh
        valid[(g >= 0) & (g < TSEQ)] = 1.0
        cb_col = np.ascontiguousarray(c_[b].reshape(8, 128).T)
        in_maps.append({
            "x": np.ascontiguousarray(xs), "valid": np.ascontiguousarray(valid),
            "qw": qw_blk, "qb": qb_col, "kvw": kvw_blk, "bvr": bvr,
            "ow": ow_c, "ob": ob_col, "upt": upt, "gat": gat, "dnt": dnt,
            "rt": rt, "wm": wm, "cb": cb_col,
        })
    return in_maps


def kernel(**inputs):
    in_maps = _stage_inputs(**inputs)
    nc = build_nc("full")
    res = run_bass_kernel_spmd(nc, in_maps, core_ids=list(range(8)))
    out = np.zeros((4, TSEQ, C), np.float32)
    for core in range(8):
        b, par = core // 2, core % 2
        oc = res.results[core]["out"]
        for i in range(OWN):
            g0 = (par + 2 * i) * 512
            out[b, g0:g0 + 512, :] = oc[:, 512 * i:512 * i + 512].T
    return out


# version 25 (final)
# speedup vs baseline: 1.1139x; 1.1139x over previous
import math
import contextlib
import numpy as np

import concourse.bass as bass
import concourse.mybir as mybir
from concourse.bass_utils import run_bass_kernel_spmd

F32 = mybir.dt.float32
BF16 = mybir.dt.bfloat16
AF = mybir.ActivationFunctionType
ALU = mybir.AluOpType
AXX = mybir.AxisListType.X

C = 1024
CC = 8
TSEQ = 4096
NH = 16
HPAIRS = 8
EXP = 8
NE = 9
H = 2048
HHN = 16
PAD = 2048
TL = PAD + TSEQ
KT_MAX = 44
TT_N = 11
OWN = 4
OWNT = 2048
SLOPES = [(h + 1) / NH for h in range(NH)]
WT = [min(16, math.ceil(14.75 / (h + 1))) for h in range(NH)]
NEG = -30000.0


def build_nc(stage="full"):
    nc = bass.Bass("TRN2", target_bir_lowering=False)

    dx = nc.dram_tensor("x", [128, TL // 512, CC, 512], F32, kind="ExternalInput").ap()
    dvalid = nc.dram_tensor("valid", [128, 48], F32, kind="ExternalInput").ap()
    dqw = nc.dram_tensor("qw", [HPAIRS, 128, CC, 128], F32, kind="ExternalInput").ap()
    dqb = nc.dram_tensor("qb", [128, 8], F32, kind="ExternalInput").ap()
    dkvw = nc.dram_tensor("kvw", [HPAIRS, 128, CC, 256], F32, kind="ExternalInput").ap()
    dbvr = nc.dram_tensor("bvr", [1, C], F32, kind="ExternalInput").ap()
    dow = nc.dram_tensor("ow", [128, CC, C], F32, kind="ExternalInput").ap()
    dob = nc.dram_tensor("ob", [128, 8], F32, kind="ExternalInput").ap()
    dupt = nc.dram_tensor("upt", [NE, HHN, 128, CC, 128], F32, kind="ExternalInput").ap()
    dgat = nc.dram_tensor("gat", [NE, HHN, 128, CC, 128], F32, kind="ExternalInput").ap()
    ddnt = nc.dram_tensor("dnt", [NE, 128, HHN, C], F32, kind="ExternalInput").ap()
    drt = nc.dram_tensor("rt", [C, 8], F32, kind="ExternalInput").ap()
    dwm = nc.dram_tensor("wm", [128, 12, CC, 512], F32, kind="ExternalInput").ap()
    dcb = nc.dram_tensor("cb", [128, 8], F32, kind="ExternalInput").ap()
    dout = nc.dram_tensor("out", [C, OWNT], F32, kind="ExternalOutput").ap()
    dscr = nc.dram_tensor("scr_mod", [6 * C], F32, kind="Internal").ap()
    dmask = nc.dram_tensor("scr_mask", [8, OWNT], F32, kind="Internal").ap()

    dbg = {}
    if stage == "h1":
        dbg["h1"] = nc.dram_tensor("dbg_h1", [128, CC, TL - PAD], BF16, kind="ExternalOutput").ap()
        dbg["h1p"] = nc.dram_tensor("dbg_h1p", [128, CC, PAD], BF16, kind="ExternalOutput").ap()
    elif stage == "mod":
        dbg["modv"] = nc.dram_tensor("dbg_modv", [128, 48], F32, kind="ExternalOutput").ap()
        dbg["modd"] = nc.dram_tensor("dbg_modd", [128, 72], F32, kind="ExternalOutput").ap()
        dbg["eqb"] = nc.dram_tensor("dbg_eqb", [128, 8], F32, kind="ExternalOutput").ap()
        dbg["mtm"] = nc.dram_tensor("dbg_mtm", [128, 8], F32, kind="ExternalOutput").ap()
    elif stage == "attn":
        dbg["yall"] = nc.dram_tensor("dbg_yall", [128, CC, OWNT], BF16, kind="ExternalOutput").ap()
    elif stage == "fdbg":
        dbg["dmask"] = nc.dram_tensor("dbg_dmask", [8, OWNT], F32, kind="ExternalOutput").ap()
        dbg["lg"] = nc.dram_tensor("dbg_lg", [128, 4, 32], F32, kind="ExternalOutput").ap()
    elif stage == "x1":
        dbg["x1"] = nc.dram_tensor("dbg_x1", [128, CC, OWNT], F32, kind="ExternalOutput").ap()
        dbg["h2"] = nc.dram_tensor("dbg_h2", [128, CC, OWNT], BF16, kind="ExternalOutput").ap()
        dbg["mrows"] = nc.dram_tensor("dbg_mrows", [8, OWNT], BF16, kind="ExternalOutput").ap()

    es = contextlib.ExitStack()
    with es:
        E = es.enter_context

        sems = {}
        for nm in ("sy", "gd", "gc", "pe", "ac", "ve", "mr", "m1", "dk",
                   "u0", "u1", "u2", "u3", "dn", "pw0", "pw1", "xa", "xb", "xq"):
            sems[nm] = [E(nc.semaphore("s_" + nm)), 0]

        def inc(which, instr, k=1):
            instr.then_inc(sems[which][0], k)
            sems[which][1] += k
            return sems[which][1]

        def cnt(which):
            return sems[which][1]

        def wait(eng, which, n=None):
            n = sems[which][1] if n is None else n
            if n > 0:
                eng.wait_ge(sems[which][0], n)

        TE, VE, SE, GE, SY = nc.tensor, nc.vector, nc.scalar, nc.gpsimd, nc.sync

        h2t = E(nc.sbuf_tensor("h2t", [128, CC, PAD], BF16))
        modv = E(nc.sbuf_tensor("modv", [128, 48], F32))
        modd = E(nc.sbuf_tensor("modd", [128, 72], F32))

        silu_c = E(nc.sbuf_tensor("silu_c", [128, 8], F32))
        cbs = E(nc.sbuf_tensor("cbs", [128, 8], F32))
        qbs = E(nc.sbuf_tensor("qbs", [128, 8], F32))
        qbs2 = E(nc.sbuf_tensor("qbs2", [128, 8], F32))
        obs = E(nc.sbuf_tensor("obs", [128, 8], F32))
        rts = E(nc.sbuf_tensor("rts", [128, CC, 8], F32))
        bvr_b = E(nc.sbuf_tensor("bvr_b", [1, C], BF16))
        ones_row = E(nc.sbuf_tensor("ones_row", [1, 512], F32))
        ones_row_b = E(nc.sbuf_tensor("ones_row_b", [1, 512], BF16))
        ones_col = E(nc.sbuf_tensor("ones_col", [128, 1], F32))
        eps1 = E(nc.sbuf_tensor("eps1", [1, 1], F32))
        iota_p = E(nc.sbuf_tensor("iota_p", [128, 1], F32))
        idn32 = E(nc.sbuf_tensor("idn32", [128, 128], F32))
        r48 = E(nc.sbuf_tensor("r48", [48, 128], F32))
        bv_off = []
        bv_tot = 0
        for h in range(NH):
            bv_off.append(bv_tot)
            bv_tot += max(WT[h], 3)
        GOFF = [0, 128, 384, 768]
        st_mean = E(nc.sbuf_tensor("st_mean", [1, 512], F32))
        st_msq = E(nc.sbuf_tensor("st_msq", [1, 512], F32))
        st_rstd = E(nc.sbuf_tensor("st_rstd", [1, 512], F32))
        sgbuf = E(nc.sbuf_tensor("sgbuf", [128, 2, 512], BF16))
        tmpf = E(nc.sbuf_tensor("tmpf", [128, 2, 512], F32))
        eqb = E(nc.sbuf_tensor("eqb", [128, 8], F32))
        remb = E(nc.sbuf_tensor("remb", [128, 1], F32))
        rmaxb = E(nc.sbuf_tensor("rmaxb", [128, 1], F32))
        mtm = E(nc.sbuf_tensor("mtm", [128, 8], F32))
        z8 = E(nc.sbuf_tensor("z8", [128, 8], F32))

        PB = [E(nc.psum_tensor(f"pb{i}", [128, 512], F32)) for i in range(8)]
        bank_free = {i: None for i in range(8)}

        def te_claim(b):
            if bank_free[b] is not None:
                which, n = bank_free[b]
                if n > 0:
                    TE.wait_ge(sems[which][0], n)
                bank_free[b] = None

        def release(b, which):
            bank_free[b] = (which, cnt(which))

        guard = {}

        def gwait(eng, key):
            if key in guard:
                which, n = guard[key]
                if n > 0:
                    eng.wait_ge(sems[which][0], n)
                del guard[key]

        def gset(key, which):
            guard[key] = (which, cnt(which))

        def vsync(instr):
            inc("ve", instr, 1)
            wait(VE, "ve")
            return instr

        CA1, NCA1, NSACA1, CB1, NCB1, NSBCB1, GA, GB, CGA = 0, 8, 16, 24, 32, 40, 48, 56, 64

        es_rb = contextlib.ExitStack()
        Erb = es_rb.enter_context
        ciota = Erb(nc.sbuf_tensor("ciota", [128, 512], F32, side="right"))
        Dm = [Erb(nc.sbuf_tensor(f"Dm_{c4}", [128, 128 * (c4 + 1)], BF16, side="right")) for c4 in range(4)]
        bv_all = Erb(nc.sbuf_tensor("bv_all", [128, bv_tot], F32, side="right"))
        valid_s = Erb(nc.sbuf_tensor("valid_s", [128, 48], F32, side="right"))
        es_rb2 = contextlib.ExitStack()
        x2buf = es_rb2.enter_context(nc.sbuf_tensor("x2buf", [128, 2, 512], F32, side="right"))
        tbuf = es_rb2.enter_context(nc.sbuf_tensor("tbuf", [128, 2, 512], F32, side="right"))

        inc("sy", SY.dma_start(out=valid_s[:], in_=dvalid), 16)
        inc("sy", SY.dma_start(out=cbs[:], in_=dcb), 16)
        inc("sy", SY.dma_start(out=qbs[:], in_=dqb), 16)
        inc("sy", SY.dma_start(out=obs[:], in_=dob), 16)
        inc("sy", SY.dma_start(out=rts[:], in_=drt.rearrange("(c p) e -> p c e", c=CC, p=128)), 16)
        inc("gd", GE.dma_start(out=bvr_b[:], in_=dbvr), 16)
        n_small_loads = cnt("sy")

        GE.memset(z8[:], 0.0)
        GE.memset(ones_row[:], 1.0)
        GE.memset(ones_row_b[:], 1.0)
        GE.memset(ones_col[:], 1.0)
        GE.memset(eps1[:], 1e-5)
        GE.iota(iota_p[:], pattern=[[0, 1]], base=0, channel_multiplier=1,
                allow_small_or_imprecise_dtypes=True)
        GE.iota(ciota[:], pattern=[[1, 512]], base=0, channel_multiplier=0,
                allow_small_or_imprecise_dtypes=True)
        GE.memset(idn32[:], 1.0)
        GE.affine_select(idn32[:], idn32[:], pattern=[[-1, 128]], compare_op=ALU.is_equal,
                         fill=0.0, channel_multiplier=1)
        for c4 in range(4):
            GE.iota(Dm[c4][:], pattern=[[128, c4 + 1], [-1, 128]], base=-128 * c4,
                    channel_multiplier=1, allow_small_or_imprecise_dtypes=True)
            ins = GE.affine_select(Dm[c4][:], Dm[c4][:], pattern=[[-128, c4 + 1], [1, 128]],
                                   base=128 * c4, compare_op=ALU.is_ge, fill=NEG,
                                   channel_multiplier=-1)
        inc("gc", ins, 1)

        wait(VE, "sy", n_small_loads)
        wait(VE, "gc")
        VE.tensor_scalar(qbs2[:], qbs[:], 0.125, 0.0, ALU.mult, ALU.add)
        for h in range(NH):
            for d in range(1, max(WT[h], 3) + 1):
                VE.tensor_scalar(bv_all[:, bv_off[h] + d - 1: bv_off[h] + d], iota_p[:],
                                 SLOPES[h], -128.0 * d * SLOPES[h], ALU.mult, ALU.add)
        wait(SE, "sy", n_small_loads)
        inc("ac", SE.activation(out=silu_c[:], in_=cbs[:], func=AF.Silu), 1)

        es_r3 = contextlib.ExitStack()
        XBUF = [es_r3.enter_context(nc.sbuf_tensor(f"xbuf{i}", [128, CC, 512], F32, side="right")) for i in range(2)]

        for j in range(12):
            b = j % 2
            xsem = "xa" if b == 0 else "xb"
            gwait(SY, f"xbuf{b}")
            inc(xsem, SY.dma_start(out=XBUF[b][:], in_=dwm[:, j]), 16)
            wait(TE, xsem)
            wait(TE, "ac")
            te_claim(4)
            for cc in range(CC):
                ins = TE.matmul(PB[4][0:1, :], silu_c[:, cc:cc + 1], XBUF[b][:, cc, :],
                                start=(cc == 0), stop=(cc == CC - 1))
            inc("pe", ins, 1)
            gset(f"xbuf{b}", "pe")
            wait(SE, "pe")
            gwait(SE, "modrow1")
            inc("ac", SE.copy(out=st_mean[:], in_=PB[4][0:1, :]), 1)
            release(4, "ac")
            wait(GE, "ac")
            inc("gd", GE.dma_start(out=dscr[512 * j: 512 * (j + 1)].rearrange("(o a) -> o a", o=1),
                                   in_=st_mean[:]), 16)
            gset("modrow1", "gd")
        wait(GE, "gd")
        inc("gd", GE.dma_start(out=r48[:], in_=dscr.rearrange("(a b) -> a b", a=48, b=128)), 16)
        wait(TE, "gd")
        wait(TE, "gc")
        te_claim(5)
        inc("pe", TE.transpose(PB[5][:, 0:48], r48[:], idn32[0:48, 0:48]), 1)
        wait(SE, "pe")
        inc("ac", SE.copy(out=modv[:], in_=PB[5][:, 0:48]), 1)
        release(5, "ac")

        wait(VE, "ac")
        wait(VE, "sy", n_small_loads)
        vsync(VE.tensor_scalar(modd[:, CA1:CA1 + 8], modv[:, 8:16], 1.0, 1.0, ALU.mult, ALU.add))
        vsync(VE.tensor_scalar(modd[:, NCA1:NCA1 + 8], modd[:, CA1:CA1 + 8], -1.0, 0.0, ALU.mult, ALU.add))
        vsync(VE.tensor_scalar(modd[:, CB1:CB1 + 8], modv[:, 32:40], 1.0, 1.0, ALU.mult, ALU.add))
        vsync(VE.tensor_scalar(modd[:, NCB1:NCB1 + 8], modd[:, CB1:CB1 + 8], -1.0, 0.0, ALU.mult, ALU.add))
        for (src, dst) in ((16, GA), (40, GB)):
            vsync(VE.tensor_scalar(modd[:, dst:dst + 8], modv[:, src:src + 8], -1.0, 0.0, ALU.mult, ALU.add))
            vsync(VE.tensor_tensor(modd[:, dst:dst + 8], modd[:, dst:dst + 8], modv[:, src:src + 8], ALU.max))
            vsync(VE.tensor_scalar(modd[:, dst:dst + 8], modd[:, dst:dst + 8], 0.25, 0.0, ALU.max, ALU.add))
            vsync(VE.tensor_tensor(eqb[:], modv[:, src:src + 8], z8[:], ALU.is_ge))
            vsync(VE.tensor_scalar(mtm[:], eqb[:], 2.0, -1.0, ALU.mult, ALU.add))
            vsync(VE.tensor_tensor(modd[:, dst:dst + 8], modd[:, dst:dst + 8], mtm[:], ALU.mult))
        vsync(VE.tensor_tensor(modd[:, CGA:CGA + 8], modd[:, GA:GA + 8], obs[:], ALU.mult))

        if stage == "mod":
            wait(SY, "ve")
            inc("sy", SY.dma_start(out=dbg["modv"], in_=modv[:]), 16)
            inc("sy", SY.dma_start(out=dbg["modd"], in_=modd[:]), 16)
            inc("sy", SY.dma_start(out=dbg["eqb"], in_=eqb[:]), 16)
            inc("sy", SY.dma_start(out=dbg["mtm"], in_=mtm[:]), 16)
            wait(GE, "sy")
            es_r3.close()
            return nc

        def emit_ln_h(src_tile, dst_fn, ca1_col, sh_col, h2f32_dst=None, post_cc=None):
            te_claim(0)
            te_claim(1)
            for cc in range(CC):
                b = cc % 2
                gwait(SE, f"x2_{b}")
                inc("ac", SE.activation(out=x2buf[:, b, :], in_=src_tile(cc), func=AF.Square), 1)
                ins = TE.matmul(PB[0][0:1, :], ones_col[:], src_tile(cc),
                                start=(cc == 0), stop=(cc == CC - 1))
                if cc == CC - 1:
                    inc("pe", ins, 1)
                wait(TE, "ac")
                ins = TE.matmul(PB[1][0:1, :], ones_col[:], x2buf[:, b, :],
                                start=(cc == 0), stop=(cc == CC - 1))
                inc("pe", ins, 1)
                gset(f"x2_{b}", "pe")
            wait(VE, "pe")
            vsync(VE.tensor_scalar(st_mean[:], PB[0][0:1, :], 1.0 / C, 0.0, ALU.mult, ALU.add))
            vsync(VE.tensor_tensor(st_msq[:], st_mean[:], st_mean[:], ALU.mult))
            ins = VE.scalar_tensor_tensor(st_msq[:], PB[1][0:1, :], 1.0 / C, st_msq[:],
                                          ALU.mult, ALU.subtract)
            inc("ve", ins, 1)
            release(0, "ve")
            release(1, "ve")
            wait(SE, "ve")
            SE.activation(out=st_msq[:], in_=st_msq[:], func=AF.Ln, bias=eps1[:])
            ins = SE.activation(out=st_rstd[:], in_=st_msq[:], func=AF.Exp, scale=-0.5)
            inc("ac", ins, 1)
            wait(TE, "ac")
            te_claim(2)
            te_claim(3)
            inc("pe", TE.matmul(PB[2][:], ones_row[:, 0:128], st_rstd[:], start=True, stop=True), 1)
            inc("pe", TE.matmul(PB[3][:], ones_row[:, 0:128], st_mean[:], start=True, stop=True), 1)
            for cc in range(CC):
                wait(VE, "pe")
                b = cc % 2
                VE.tensor_tensor(tbuf[:, b, :], src_tile(cc), PB[3][:], ALU.subtract)
                VE.scalar_tensor_tensor(tbuf[:, 1 - b, :], tbuf[:, b, :],
                                        modd[:, ca1_col + cc:ca1_col + cc + 1],
                                        PB[2][:], ALU.mult, ALU.mult)
                if h2f32_dst is not None:
                    VE.tensor_scalar(h2f32_dst(cc), tbuf[:, 1 - b, :],
                                     modv[:, sh_col + cc:sh_col + cc + 1], 0.0, ALU.add, ALU.add)
                    ins = VE.tensor_copy(dst_fn(cc), h2f32_dst(cc))
                    inc("ve", ins, 1)
                else:
                    ins = VE.tensor_scalar(dst_fn(cc), tbuf[:, 1 - b, :],
                                           modv[:, sh_col + cc:sh_col + cc + 1], 0.0, ALU.add, ALU.add)
                    inc("ve", ins, 1)
                if post_cc is not None:
                    post_cc(cc)
            release(2, "ve")
            release(3, "ve")

        es_r2 = contextlib.ExitStack()
        h1r = es_r2.enter_context(nc.sbuf_tensor("h1r", [128, CC, TL - PAD], BF16))

        def h1_slice(cc, u0, w):
            if u0 < PAD:
                return h2t[:, cc, u0:u0 + w]
            return h1r[:, cc, u0 - PAD:u0 - PAD + w]

        for tt in range(3, TT_N):
            b = tt % 2
            u0 = tt * 512
            xsem = "xa" if b == 0 else "xb"
            gwait(SY, f"xbuf{b}")
            inc(xsem, SY.dma_start(out=XBUF[b][:], in_=dx[:, tt]), 16)
            wait(TE, xsem)
            wait(SE, xsem)
            wait(VE, xsem)

            def src_tile(cc, _b=b):
                return XBUF[_b][:, cc, :]

            def dst_fn(cc, _u0=u0):
                return h1_slice(cc, _u0, 512)

            emit_ln_h(src_tile, dst_fn, CA1, 0)
            gset(f"xbuf{b}", "ve")

        if stage == "h1":
            wait(SY, "ve")
            inc("sy", SY.dma_start(out=dbg["h1"], in_=h1r[:]), 16)
            inc("sy", SY.dma_start(out=dbg["h1p"], in_=h2t[:]), 16)
            wait(GE, "sy")
            es_r3.close(); es_r2.close()
            return nc

        es_r3.close()
        es_rb2.close()
        es_r4 = contextlib.ExitStack()
        y_all = es_r4.enter_context(nc.sbuf_tensor("y_all", [128, CC, OWNT], BF16, side="right"))
        k2 = es_r3.enter_context(nc.sbuf_tensor("k2", [128, TT_N * 512], BF16, side="right"))
        q2 = es_r3.enter_context(nc.sbuf_tensor("q2", [128, OWN, 512], BF16, side="right"))
        v2 = es_r3.enter_context(nc.sbuf_tensor("v2", [128, KT_MAX, 130], BF16, side="right"))
        Gcur = es_r3.enter_context(nc.sbuf_tensor("Gcur", [128, 1280], BF16, side="right"))
        colf = es_r3.enter_context(nc.sbuf_tensor("colf", [128, 2, 512], BF16, side="right"))
        qwt = es_r3.enter_context(nc.sbuf_tensor("qwt", [128, 2, CC, 128], BF16, side="right"))
        kvwt = es_r3.enter_context(nc.sbuf_tensor("kvwt", [128, 2, CC, 256], BF16, side="right"))
        e1buf = es_r3.enter_context(nc.sbuf_tensor("e1buf", [128, 3, 512], BF16, side="right"))
        pbuf = es_r3.enter_context(nc.sbuf_tensor("pbuf", [128, 3, 512], BF16, side="right"))
        recip_rb = es_r3.enter_context(nc.sbuf_tensor("recip_rb", [1, 512], BF16, side="right"))
        srow = st_mean
        recip_r = st_msq
        srow_b = sgbuf[:, 1, :]
        recips = sgbuf[:, 0, :]

        wait(GE, "ve")
        wait(GE, "pe")
        inc("gc", GE.memset(v2[:], 1.0), 1)
        inc("gc", GE.memset(k2[:, 0:1536], 0.0), 1)

        norm_pending = [None]
        yb_idx = [0]

        for hp in range(HPAIRS):
            if norm_pending[0] is not None:
                norm_pending[0][0]()
                norm_pending[0][1]()
                norm_pending[0] = None
            wslot = hp % 2
            psem = "pw0" if wslot == 0 else "pw1"
            if hp == 0:
                gwait(GE, f"pairw_{wslot}")
                inc(psem, GE.dma_start(out=qwt[:, wslot], in_=dqw[hp]), 16)
                inc(psem, GE.dma_start(out=kvwt[:, wslot], in_=dkvw[hp]), 16)
            wait(TE, psem)
            wait(TE, "gc")

            for tt in range(3, TT_N):
                u0 = tt * 512
                kb = 4 + tt % 2
                te_claim(kb)
                for cc in range(CC):
                    ins = TE.matmul(PB[kb][:], kvwt[:, wslot, cc, 0:128], h1_slice(cc, u0, 512),
                                    start=(cc == 0), stop=(cc == CC - 1))
                inc("pe", ins, 1)
                wait(SE, "pe")
                inc("ac", SE.copy(out=k2[:, u0:u0 + 512], in_=PB[kb][:]), 1)
                release(kb, "ac")
            for qi in range(OWN):
                off = qi * 1024
                kb = 4 + qi % 2
                te_claim(kb)
                for cc in range(CC):
                    ins = TE.matmul(PB[kb][:], qwt[:, wslot, cc, :], h1r[:, cc, off:off + 512],
                                    start=(cc == 0), stop=(cc == CC - 1))
                inc("pe", ins, 1)
                wait(SE, "pe")
                inc("ac", SE.activation(out=q2[:, qi, :], in_=PB[kb][:], func=AF.Identity,
                                        scale=0.125, bias=qbs2[:, hp:hp + 1]), 1)
                release(kb, "ac")
            for kt in range(12, KT_MAX):
                u0 = kt * 128
                vb = 3 if kt % 2 == 0 else 6
                te_claim(vb)
                for cc in range(CC):
                    ins = TE.matmul(PB[vb][:, 0:128], h1_slice(cc, u0, 128), kvwt[:, wslot, cc, 128:256],
                                    start=(cc == 0), stop=(cc == CC - 1))
                inc("pe", ins, 1)
                wait(VE, "pe")
                VE.tensor_copy(v2[:, kt, 0:64], PB[vb][:, 0:64])
                ins = VE.tensor_copy(v2[:, kt, 65:129], PB[vb][:, 64:128])
                inc("ve", ins, 1)
                release(vb, "ve")
            if hp + 1 < HPAIRS:
                ns = (hp + 1) % 2
                nsem = "pw0" if ns == 0 else "pw1"
                gwait(GE, f"pairw_{ns}")
                inc(nsem, GE.dma_start(out=qwt[:, ns], in_=dqw[hp + 1]), 16)
                inc(nsem, GE.dma_start(out=kvwt[:, ns], in_=dkvw[hp + 1]), 16)
            gset(f"pairw_{wslot}", "pe")

            wait(SE, "ve")
            for hip in range(2):
                ins = SE.activation(out=colf[:, hip, :], in_=ciota[:], func=AF.Exp,
                                    scale=-SLOPES[2 * hp + hip])
            inc("ac", ins, 1)
            wait(TE, "ac")

            for hip in range(2):
                h = 2 * hp + hip
                r0 = 64 * hip
                vlo, vhi = (0, 65) if hip == 0 else (65, 130)
                gwait(SE, "gcur")
                for c4 in range(4):
                    ins = SE.activation(out=Gcur[:, GOFF[c4]:GOFF[c4] + 128 * (c4 + 1)],
                                        in_=Dm[c4][:], func=AF.Exp, scale=SLOPES[h])
                inc("ac", ins, 1)
                for qi in range(OWN):
                    q0t = (PAD + qi * 1024) // 128
                    yb = 2 + (yb_idx[0] % 2)
                    yb_idx[0] += 1
                    slots = [("off", q0t - dlt, dlt) for dlt in range(WT[h], 0, -1)]
                    slots += [("diag", c4) for c4 in range(4)]
                    n_s = len(slots)
                    av_ready = {}

                    SBMAP = (0, 1, 4)

                    def emit_front(i):
                        kind = slots[i][0]
                        r = i % 3
                        sb = SBMAP[r]
                        te_claim(sb)
                        if kind == "off":
                            _, kt, dlt = slots[i]
                            s_rdy = inc("pe", TE.matmul(PB[sb][:], k2[r0:r0 + 64, kt * 128:kt * 128 + 128],
                                                        q2[r0:r0 + 64, qi, :], start=True, stop=True), 1)
                            wait(SE, "pe", s_rdy)
                            gwait(SE, f"e1_{r}")
                            inc("ac", SE.activation(out=e1buf[:, r, :], in_=PB[sb][:], func=AF.Exp,
                                                    bias=bv_all[:, bv_off[h] + dlt - 1:bv_off[h] + dlt]), 1)
                            release(sb, "ac")
                            gwait(VE, f"pb_{r}")
                            wait(VE, "ac")
                            ins = VE.scalar_tensor_tensor(pbuf[:, r, :], e1buf[:, r, :],
                                                          valid_s[:, kt:kt + 1], colf[:, hip, :],
                                                          ALU.mult, ALU.mult)
                            inc("ve", ins, 1)
                            gset(f"e1_{r}", "ve")
                        else:
                            _, c4 = slots[i]
                            w = 128 * (c4 + 1)
                            for j4 in range(c4 + 1):
                                kt = q0t + j4
                                ins = TE.matmul(PB[sb][:, 128 * j4:128 * j4 + 128],
                                                k2[r0:r0 + 64, kt * 128:kt * 128 + 128],
                                                q2[r0:r0 + 64, qi, 128 * c4:128 * c4 + 128],
                                                start=True, stop=True)
                            s_rdy = inc("pe", ins, 1)
                            wait(SE, "pe", s_rdy)
                            gwait(SE, f"e1_{r}")
                            inc("ac", SE.activation(out=e1buf[:, r, 0:w], in_=PB[sb][:, 0:w],
                                                    func=AF.Exp), 1)
                            release(sb, "ac")
                            gwait(VE, f"pb_{r}")
                            wait(VE, "ac")
                            ins = VE.tensor_tensor(pbuf[:, r, 0:w], e1buf[:, r, 0:w],
                                                   Gcur[:, GOFF[c4]:GOFF[c4] + w], ALU.mult)
                            inc("ve", ins, 1)
                            gset(f"e1_{r}", "ve")
                        av_ready[i] = cnt("ve")

                    def emit_av(i, _yb=yb, _q0t=q0t):
                        kind = slots[i][0]
                        r = i % 3
                        wait(TE, "ve", av_ready[i])
                        if kind == "off":
                            _, kt, dlt = slots[i]
                            inc("pe", TE.matmul(PB[_yb][0:65, :], v2[:, kt, vlo:vhi],
                                                pbuf[:, r, :], start=(i == 0), stop=False), 1)
                        else:
                            _, c4 = slots[i]
                            for j4 in range(c4 + 1):
                                kt = _q0t + j4
                                ins = TE.matmul(PB[_yb][0:65, 128 * c4:128 * c4 + 128],
                                                v2[:, kt, vlo:vhi],
                                                pbuf[:, r, 128 * j4:128 * j4 + 128],
                                                start=False, stop=False)
                            inc("pe", ins, 1)
                        gset(f"pb_{r}", "pe")

                    te_claim(yb)
                    for i in range(n_s + 1):
                        if i < n_s:
                            emit_front(i)
                        if i == 1 and norm_pending[0] is not None:
                            norm_pending[0][0]()
                        if i == min(5, n_s) and norm_pending[0] is not None:
                            norm_pending[0][1]()
                            norm_pending[0] = None
                        if i >= 1:
                            emit_av(i - 1)
                    av_last = cnt("pe")

                    def make_norm(_yb=yb, _h=h, _qi=qi, _av=None):
                        st = {}

                        def norm_dve():
                            wait(VE, "pe", _av)
                            vsync(VE.tensor_copy(srow[:], PB[_yb][64:65, :]))
                            VE.tensor_copy(srow_b[0:1, :], srow[:])
                            vsync(VE.reciprocal(recip_r[:], srow[:]))
                            ins = VE.tensor_copy(recip_rb[:], recip_r[:])
                            inc("ve", ins, 1)
                            st["dve"] = cnt("ve")

                        def norm_te():
                            wait(TE, "ve", st["dve"])
                            inc("pe", TE.matmul(PB[_yb][0:64, :], bvr_b[:, 64 * _h:64 * _h + 64],
                                                srow_b[0:1, :], start=False, stop=True), 1)
                            te_claim(7)
                            inc("pe", TE.matmul(PB[7][0:64, :], ones_row_b[:, 0:64], recip_rb[:],
                                                start=True, stop=True), 1)
                            wait(VE, "pe")
                            VE.tensor_copy(recips[0:64, :], PB[7][0:64, :])
                            ccv = _h // 2
                            rr = 64 * (_h % 2)
                            ins = VE.tensor_tensor(y_all[rr:rr + 64, ccv, _qi * 512:_qi * 512 + 512],
                                                   PB[_yb][0:64, :], recips[0:64, :], ALU.mult)
                            inc("ve", ins, 1)
                            release(_yb, "ve")
                            release(7, "ve")
                        return (norm_dve, norm_te)

                    norm_pending[0] = make_norm(_av=av_last)
                gset("gcur", "ve")
        if norm_pending[0] is not None:
            norm_pending[0][0]()
            norm_pending[0][1]()
            norm_pending[0] = None

        if stage == "attn":
            wait(SY, "ve")
            inc("sy", SY.dma_start(out=dbg["yall"], in_=y_all[:]), 16)
            wait(GE, "sy")
            es_r3.close(); es_r4.close(); es_r2.close()
            return nc

        es_r3.close()
        x2buf = es_r3.enter_context(nc.sbuf_tensor("x2buf_p3", [128, 2, 512], F32, side="right"))
        tbuf = es_r3.enter_context(nc.sbuf_tensor("tbuf_p3", [128, 2, 512], F32, side="right"))
        owt = es_r3.enter_context(nc.sbuf_tensor("owt", [128, CC, C], BF16, side="right"))
        xq2 = es_r3.enter_context(nc.sbuf_tensor("xq2", [128, 2, 512], F32, side="right"))
        h2f_c = es_r3.enter_context(nc.sbuf_tensor("h2f_c", [128, 512], F32, side="right"))
        logits_sb = es_r3.enter_context(nc.sbuf_tensor("logits_sb", [128, 32], F32, side="right"))
        es_r2.close()
        yacc = es_r2.enter_context(nc.sbuf_tensor("yacc", [128, CC, OWNT], F32))
        m8buf = es_r2.enter_context(nc.sbuf_tensor("m8buf", [8, 128], F32))
        wait(GE, "pe")
        wait(GE, "ve")
        inc("gd", GE.dma_start(out=owt[:], in_=dow), 16)

        lgdump = None
        if stage == "fdbg":
            lgdump = es_r2.enter_context(nc.sbuf_tensor("lgdump", [128, 4, 32], F32))
        first_p3 = True
        for qi in range(OWN):
            u0 = PAD + qi * 1024
            if first_p3:
                wait(SY, "pe")
                wait(SY, "ve")
                wait(TE, "ve")
                first_p3 = False
            for cco in range(CC):
                ob_ = cco % 2
                gwait(SY, f"xq_{ob_}")
                inc("xq", SY.dma_start(out=xq2[:, ob_, :], in_=dx[:, u0 // 512, cco]), 16)
                wait(TE, "gd")
                te_claim(6 + ob_)
                for cci in range(CC):
                    ins = TE.matmul(PB[6 + ob_][:], owt[:, cci, 128 * cco:128 * cco + 128],
                                    y_all[:, cci, qi * 512:qi * 512 + 512],
                                    start=(cci == 0), stop=(cci == CC - 1))
                inc("pe", ins, 1)
                wait(VE, "pe")
                wait(VE, "xq")
                VE.scalar_tensor_tensor(tmpf[:, ob_, :], PB[6 + ob_][:], modd[:, GA + cco:GA + cco + 1],
                                        xq2[:, ob_, :], ALU.mult, ALU.add)
                ins = VE.tensor_scalar(yacc[:, cco, qi * 512:qi * 512 + 512], tmpf[:, ob_, :],
                                       modd[:, CGA + cco:CGA + cco + 1], 0.0, ALU.add, ALU.add)
                inc("ve", ins, 1)
                release(6 + ob_, "ve")
                gset(f"xq_{ob_}", "ve")

            def src2(cc, _qi=qi):
                return yacc[:, cc, _qi * 512:_qi * 512 + 512]

            def dst2(cc, _qi=qi):
                return h2t[:, cc, _qi * 512:_qi * 512 + 512]

            def dsth2f(cc):
                return h2f_c[:]

            def post_cc(cc):
                te_claim(5)
                wait(TE, "ve")
                for sub in range(4):
                    ins = TE.matmul(PB[5][:, 8 * sub:8 * sub + 8],
                                    h2f_c[:, 128 * sub:128 * sub + 128], rts[:, cc, :],
                                    start=True, stop=True)
                inc("pe", ins, 1)
                wait(VE, "pe")
                if cc == 0:
                    vsync(VE.tensor_copy(logits_sb[:], PB[5][:, 0:32]))
                else:
                    vsync(VE.tensor_tensor(logits_sb[:], logits_sb[:], PB[5][:, 0:32], ALU.add))
                release(5, "ve")

            wait(TE, "ve")
            wait(SE, "ve")
            emit_ln_h(src2, dst2, CB1, 24, h2f32_dst=dsth2f, post_cc=post_cc)

            if stage == "fdbg":
                ins = VE.tensor_copy(lgdump[:, qi, :], logits_sb[:])
                inc("ve", ins, 1)
            for sub in range(4):
                wait(VE, "pe")
                vsync(VE.reduce_max(rmaxb[:], logits_sb[:, 8 * sub:8 * sub + 8], axis=AXX))
                vsync(VE.tensor_scalar(mtm[:], logits_sb[:, 8 * sub:8 * sub + 8], 0.0, rmaxb[:],
                                       ALU.mult, ALU.add))
                vsync(VE.tensor_tensor(eqb[:], logits_sb[:, 8 * sub:8 * sub + 8], mtm[:], ALU.is_ge))
                vsync(VE.memset(remb[:], 1.0))
                for e_ in range(8):
                    vsync(VE.tensor_tensor(mtm[:, e_:e_ + 1], eqb[:, e_:e_ + 1], remb[:], ALU.mult))
                    vsync(VE.scalar_tensor_tensor(remb[:], mtm[:, e_:e_ + 1], -1.0, remb[:],
                                                  ALU.mult, ALU.add))
                wait(TE, "ve")
                te_claim(4)
                inc("pe", TE.transpose(PB[4][0:8, 0:128], mtm[:], idn32[:]), 1)
                wait(VE, "pe")
                gwait(VE, "m8")
                ins = VE.tensor_copy(m8buf[:], PB[4][0:8, 0:128])
                inc("ve", ins, 1)
                release(4, "ve")
                wait(SY, "ve")
                inc("dk", SY.dma_start(out=dmask[:, qi * 512 + 128 * sub: qi * 512 + 128 * sub + 128],
                                       in_=m8buf[:]), 16)
                gset("m8", "dk")
            release(5, "ve")

        if stage == "x1":
            wait(SY, "ve")
            inc("sy", SY.dma_start(out=dbg["x1"], in_=yacc[:]), 16)
            inc("sy", SY.dma_start(out=dbg["h2"], in_=h2t[:]), 16)
            wait(GE, "sy")
            mrall = es_r3.enter_context(nc.sbuf_tensor("mrall", [8, OWNT], BF16, side="right"))
            inc("gd", GE.dma_start(out=mrall[:], in_=dmask), 16)
            wait(SY, "gd")
            inc("sy", SY.dma_start(out=dbg["mrows"], in_=mrall[:]), 16)
            wait(GE, "sy")
            es_r3.close(); es_r4.close(); es_r2.close()
            return nc

        if stage == "fdbg":
            wait(SY, "ve")
            inc("sy", SY.dma_start(out=dbg["lg"], in_=lgdump[:]), 16)
            wait(GE, "sy")
            mrall2 = es_r3.enter_context(nc.sbuf_tensor("mrall2", [8, OWNT], F32, side="right"))
            inc("gd", GE.dma_start(out=mrall2[:], in_=dmask), 16)
            wait(SY, "gd")
            inc("sy", SY.dma_start(out=dbg["dmask"], in_=mrall2[:]), 16)
            wait(GE, "sy")
            es_r3.close(); es_r4.close(); es_rb.close(); es_r2.close()
            return nc

        es_r3.close()
        es_r4.close()
        es_rb.close()
        dn_s = es_r3.enter_context(nc.sbuf_tensor("dn_s", [128, HHN, C], BF16, side="right"))
        act_b = es_r3.enter_context(nc.sbuf_tensor("act_b", [128, HHN, 1024], BF16, side="right"))
        ugb = es_r3.enter_context(nc.sbuf_tensor("ugb", [128, 4, 2, CC, 128], BF16, side="right"))
        maskb = es_r3.enter_context(nc.sbuf_tensor("maskb", [128, OWNT], BF16, side="right"))
        mrow1 = es_r3.enter_context(nc.sbuf_tensor("mrow1", [1, 2, 512], BF16, side="right"))

        wait(GE, "pe")
        wait(GE, "ve")
        wait(GE, "dk")

        def emit_mask(et):
            for sg in range(4):
                mb = sg % 2
                msem = "mr" if mb == 0 else "m1"
                gwait(GE, f"mrow_{mb}")
                inc(msem, GE.dma_start(out=mrow1[:, mb, :],
                                       in_=dmask[et:et + 1, 512 * sg:512 * sg + 512]), 16)
                wait(TE, msem)
                te_claim(6)
                inc("pe", TE.matmul(PB[6][:], ones_row_b[:, 0:128], mrow1[0:1, mb, :],
                                    start=True, stop=True), 1)
                gset(f"mrow_{mb}", "pe")
                wait(VE, "pe")
                ins = VE.tensor_copy(maskb[:, 512 * sg:512 * sg + 512], PB[6][:])
                inc("ve", ins, 1)
                release(6, "ve")

        emit_mask(0)
        ug_ready = {}
        dn_guard = 0
        for et in range(NE):
            dn_chunks = list(ddnt[et, :, 4 * ci:4 * ci + 4, :] for ci in range(4))
            for hfp in range(2):
                t0 = hfp * 1024
                for hh in range(HHN):
                    al = hh % 4
                    usem = ("u0", "u1", "u2", "u3")[al]
                    gwait(GE, f"ug_{al}")
                    inc(usem, GE.dma_start(out=ugb[:, al, 0, :, :], in_=dupt[et, hh]), 16)
                    inc(usem, GE.dma_start(out=ugb[:, al, 1, :, :], in_=dgat[et, hh]), 16)
                    if hfp == 0 and hh % 4 == 1:
                        ci = hh // 4
                        if dn_guard > 0:
                            wait(GE, "pe", dn_guard)
                        inc("dn", GE.dma_start(out=dn_s[:, 4 * ci:4 * ci + 4, :],
                                               in_=dn_chunks[ci]), 16)
                    for tf in range(2):
                        al2 = tf % 2
                        wait(TE, usem)
                        te_claim(0 + al2)
                        for cc in range(CC):
                            ins = TE.matmul(PB[0 + al2][:], ugb[:, al, 0, cc, :],
                                            h2t[:, cc, t0 + 512 * tf:t0 + 512 * tf + 512],
                                            start=(cc == 0), stop=(cc == CC - 1))
                        inc("pe", ins, 1)
                        te_claim(2 + al2)
                        for cc in range(CC):
                            ins = TE.matmul(PB[2 + al2][:], ugb[:, al, 1, cc, :],
                                            h2t[:, cc, t0 + 512 * tf:t0 + 512 * tf + 512],
                                            start=(cc == 0), stop=(cc == CC - 1))
                        inc("pe", ins, 1)
                        if tf == 1:
                            gset(f"ug_{al}", "pe")
                        wait(SE, "pe")
                        gwait(SE, f"sg_{al2}")
                        inc("ac", SE.activation(out=sgbuf[:, al2, :], in_=PB[2 + al2][:],
                                                func=AF.Silu), 1)
                        release(2 + al2, "ac")
                        wait(VE, "ac")
                        wait(VE, "pe")
                        ins = VE.tensor_tensor(act_b[:, hh, 512 * tf:512 * tf + 512],
                                               sgbuf[:, al2, :], PB[0 + al2][:], ALU.mult)
                        inc("ve", ins, 1)
                        gset(f"sg_{al2}", "ve")
                        release(0 + al2, "ve")
                if hfp == 0 and et + 1 < EXP + 1 and et + 1 < EXP:
                    pass
                wait(TE, "ve")
                wait(TE, "dn")
                for tf in range(2):
                    for cc in range(CC):
                        db = 4 + (cc + 2 * tf) % 2
                        te_claim(db)
                        for hh in range(HHN):
                            ins = TE.matmul(PB[db][:], dn_s[:, hh, 128 * cc:128 * cc + 128],
                                            act_b[:, hh, 512 * tf:512 * tf + 512],
                                            start=(hh == 0), stop=(hh == HHN - 1))
                        inc("pe", ins, 1)
                        wait(VE, "pe")
                        tt0 = t0 + 512 * tf
                        yslice = yacc[:, cc, tt0:tt0 + 512]
                        if et < EXP:
                            VE.tensor_tensor(tmpf[:, cc % 2, :], PB[db][:],
                                             maskb[:, tt0:tt0 + 512], ALU.mult)
                            ins = VE.scalar_tensor_tensor(yslice, tmpf[:, cc % 2, :],
                                                          modd[:, GB + cc:GB + cc + 1], yslice,
                                                          ALU.mult, ALU.add)
                        else:
                            ins = VE.scalar_tensor_tensor(yslice, PB[db][:],
                                                          modd[:, GB + cc:GB + cc + 1], yslice,
                                                          ALU.mult, ALU.add)
                        inc("ve", ins, 1)
                        release(db, "ve")
                if hfp == 1 and et + 1 < EXP:
                    emit_mask(et + 1)
            dn_guard = cnt("pe")

        wait(SY, "ve")
        inc("sy", SY.dma_start(out=dout.rearrange("(c p) t -> p c t", c=CC, p=128), in_=yacc[:]), 16)
        wait(GE, "sy")
        es_r3.close(); es_r2.close()
    return nc


def _stage_inputs(L0_attn_x, L0_attn_q_w, L0_attn_q_b, L0_attn_kv_w, L0_attn_kv_b,
                  L0_attn_o_w, L0_attn_o_b, L0_ffn_up, L0_ffn_gate, L0_ffn_down,
                  L0_ffn_router, L0_ffn_w_up_s, L0_ffn_w_gate_s, L0_ffn_w_down_s,
                  L0_c, L0_wmod):
    f = np.float32
    x = np.ascontiguousarray(np.asarray(L0_attn_x, dtype=f))
    q_w = np.asarray(L0_attn_q_w, f)
    q_b = np.asarray(L0_attn_q_b, f)
    kv_w = np.asarray(L0_attn_kv_w, f)
    kv_b = np.asarray(L0_attn_kv_b, f)
    o_w = np.asarray(L0_attn_o_w, f)
    ow_c = np.ascontiguousarray(o_w.reshape(CC, 128, C).transpose(1, 0, 2))
    o_b = np.asarray(L0_attn_o_b, f)
    up = np.asarray(L0_ffn_up, f)
    gate = np.asarray(L0_ffn_gate, f)
    down = np.asarray(L0_ffn_down, f)
    router = np.asarray(L0_ffn_router, f)
    wus = np.asarray(L0_ffn_w_up_s, f)
    wgs = np.asarray(L0_ffn_w_gate_s, f)
    wds = np.asarray(L0_ffn_w_down_s, f)
    c_ = np.asarray(L0_c, f)
    wm = np.asarray(L0_wmod, f)
    wm = np.ascontiguousarray(wm.reshape(CC, 128, 12, 512).transpose(1, 2, 0, 3))

    qw_blk = np.ascontiguousarray(q_w.reshape(CC, 128, HPAIRS, 128).transpose(2, 1, 0, 3))
    qb_col = np.ascontiguousarray(q_b.reshape(8, 128).T)
    kvw_blk = np.stack([np.concatenate([kv_w[:, 128 * hp:128 * hp + 128],
                                        kv_w[:, C + 128 * hp:C + 128 * hp + 128]], axis=1)
                        for hp in range(HPAIRS)])
    kvw_blk = np.ascontiguousarray(kvw_blk.reshape(HPAIRS, CC, 128, 256).transpose(0, 2, 1, 3))
    bvr = np.ascontiguousarray(kv_b[C:2 * C][None, :])
    ob_col = np.ascontiguousarray(o_b.reshape(8, 128).T)
    up_all = np.concatenate([up, wus[None]], axis=0)
    ga_all = np.concatenate([gate, wgs[None]], axis=0)
    dn_all = np.concatenate([down, wds[None]], axis=0)
    upt = np.ascontiguousarray(
        up_all.reshape(NE, HHN, 128, CC, 128).transpose(0, 1, 4, 3, 2))
    gat = np.ascontiguousarray(
        ga_all.reshape(NE, HHN, 128, CC, 128).transpose(0, 1, 4, 3, 2))
    dnt = np.ascontiguousarray(
        dn_all.transpose(0, 2, 1).reshape(NE, HHN, 128, C).transpose(0, 2, 1, 3))
    rt = np.ascontiguousarray(router.T)
    ow_c = np.ascontiguousarray(o_w.reshape(CC, 128, C).transpose(1, 0, 2))

    in_maps = []
    for core in range(8):
        b, par = core // 2, core % 2
        xs = np.zeros((C, TL), f)
        sh = PAD - 512 * par
        lo, hi = sh, min(TL, sh + TSEQ)
        xs[:, lo:hi] = x[b].T[:, 0:hi - lo]
        xs = np.ascontiguousarray(xs.reshape(CC, 128, TL // 512, 512).transpose(1, 2, 0, 3))
        valid = np.zeros((128, 48), f)
        uu = np.arange(48 * 128).reshape(48, 128).T
        g = uu - sh
        valid[(g >= 0) & (g < TSEQ)] = 1.0
        cb_col = np.ascontiguousarray(c_[b].reshape(8, 128).T)
        in_maps.append({
            "x": np.ascontiguousarray(xs), "valid": np.ascontiguousarray(valid),
            "qw": qw_blk, "qb": qb_col, "kvw": kvw_blk, "bvr": bvr,
            "ow": ow_c, "ob": ob_col, "upt": upt, "gat": gat, "dnt": dnt,
            "rt": rt, "wm": wm, "cb": cb_col,
        })
    return in_maps


def kernel(**inputs):
    in_maps = _stage_inputs(**inputs)
    nc = build_nc("full")
    res = run_bass_kernel_spmd(nc, in_maps, core_ids=list(range(8)))
    out = np.zeros((4, TSEQ, C), np.float32)
    for core in range(8):
        b, par = core // 2, core % 2
        oc = res.results[core]["out"]
        for i in range(OWN):
            g0 = (par + 2 * i) * 512
            out[b, g0:g0 + 512, :] = oc[:, 512 * i:512 * i + 512].T
    return out


# version 26
# speedup vs baseline: 1.1216x; 1.0068x over previous
import math
import contextlib
import numpy as np

import concourse.bass as bass
import concourse.mybir as mybir
from concourse.bass_utils import run_bass_kernel_spmd

F32 = mybir.dt.float32
BF16 = mybir.dt.bfloat16
AF = mybir.ActivationFunctionType
ALU = mybir.AluOpType
AXX = mybir.AxisListType.X

C = 1024
CC = 8
TSEQ = 4096
NH = 16
HPAIRS = 8
EXP = 8
NE = 9
H = 2048
HHN = 16
PAD = 2048
TL = PAD + TSEQ
KT_MAX = 44
TT_N = 11
OWN = 4
OWNT = 2048
SLOPES = [(h + 1) / NH for h in range(NH)]
WT = [min(16, math.ceil(14.75 / (h + 1))) for h in range(NH)]
NEG = -30000.0


def build_nc(stage="full"):
    nc = bass.Bass("TRN2", target_bir_lowering=False)

    dx = nc.dram_tensor("x", [128, TL // 512, CC, 512], F32, kind="ExternalInput").ap()
    dvalid = nc.dram_tensor("valid", [128, 48], F32, kind="ExternalInput").ap()
    dqw = nc.dram_tensor("qw", [HPAIRS, 128, CC, 128], F32, kind="ExternalInput").ap()
    dqb = nc.dram_tensor("qb", [128, 8], F32, kind="ExternalInput").ap()
    dkvw = nc.dram_tensor("kvw", [HPAIRS, 128, CC, 256], F32, kind="ExternalInput").ap()
    dbvr = nc.dram_tensor("bvr", [1, C], F32, kind="ExternalInput").ap()
    dow = nc.dram_tensor("ow", [128, CC, C], F32, kind="ExternalInput").ap()
    dob = nc.dram_tensor("ob", [128, 8], F32, kind="ExternalInput").ap()
    dupt = nc.dram_tensor("upt", [NE, HHN, 128, CC, 128], F32, kind="ExternalInput").ap()
    dgat = nc.dram_tensor("gat", [NE, HHN, 128, CC, 128], F32, kind="ExternalInput").ap()
    ddnt = nc.dram_tensor("dnt", [NE, 128, HHN, C], F32, kind="ExternalInput").ap()
    drt = nc.dram_tensor("rt", [C, 8], F32, kind="ExternalInput").ap()
    dwm = nc.dram_tensor("wm", [128, 12, CC, 512], F32, kind="ExternalInput").ap()
    dcb = nc.dram_tensor("cb", [128, 8], F32, kind="ExternalInput").ap()
    dout = nc.dram_tensor("out", [C, OWNT], F32, kind="ExternalOutput").ap()
    dscr = nc.dram_tensor("scr_mod", [6 * C], F32, kind="Internal").ap()
    dmask = nc.dram_tensor("scr_mask", [8, OWNT], F32, kind="Internal").ap()

    dbg = {}
    if stage == "h1":
        dbg["h1"] = nc.dram_tensor("dbg_h1", [128, CC, TL - PAD], BF16, kind="ExternalOutput").ap()
        dbg["h1p"] = nc.dram_tensor("dbg_h1p", [128, CC, PAD], BF16, kind="ExternalOutput").ap()
    elif stage == "mod":
        dbg["modv"] = nc.dram_tensor("dbg_modv", [128, 48], F32, kind="ExternalOutput").ap()
        dbg["modd"] = nc.dram_tensor("dbg_modd", [128, 72], F32, kind="ExternalOutput").ap()
        dbg["eqb"] = nc.dram_tensor("dbg_eqb", [128, 8], F32, kind="ExternalOutput").ap()
        dbg["mtm"] = nc.dram_tensor("dbg_mtm", [128, 8], F32, kind="ExternalOutput").ap()
    elif stage == "attn":
        dbg["yall"] = nc.dram_tensor("dbg_yall", [128, CC, OWNT], BF16, kind="ExternalOutput").ap()
    elif stage == "fdbg":
        dbg["dmask"] = nc.dram_tensor("dbg_dmask", [8, OWNT], F32, kind="ExternalOutput").ap()
        dbg["lg"] = nc.dram_tensor("dbg_lg", [128, 4, 32], F32, kind="ExternalOutput").ap()
    elif stage == "x1":
        dbg["x1"] = nc.dram_tensor("dbg_x1", [128, CC, OWNT], F32, kind="ExternalOutput").ap()
        dbg["h2"] = nc.dram_tensor("dbg_h2", [128, CC, OWNT], BF16, kind="ExternalOutput").ap()
        dbg["mrows"] = nc.dram_tensor("dbg_mrows", [8, OWNT], BF16, kind="ExternalOutput").ap()

    es = contextlib.ExitStack()
    with es:
        E = es.enter_context

        sems = {}
        for nm in ("sy", "gd", "gc", "pe", "ac", "ve", "mr", "m1", "dk",
                   "u0", "u1", "u2", "u3", "dn", "pw0", "pw1", "xa", "xb", "xq"):
            sems[nm] = [E(nc.semaphore("s_" + nm)), 0]

        def inc(which, instr, k=1):
            instr.then_inc(sems[which][0], k)
            sems[which][1] += k
            return sems[which][1]

        def cnt(which):
            return sems[which][1]

        def wait(eng, which, n=None):
            n = sems[which][1] if n is None else n
            if n > 0:
                eng.wait_ge(sems[which][0], n)

        TE, VE, SE, GE, SY = nc.tensor, nc.vector, nc.scalar, nc.gpsimd, nc.sync

        h2t = E(nc.sbuf_tensor("h2t", [128, CC, PAD], BF16))
        modv = E(nc.sbuf_tensor("modv", [128, 48], F32))
        modd = E(nc.sbuf_tensor("modd", [128, 72], F32))

        silu_c = E(nc.sbuf_tensor("silu_c", [128, 8], F32))
        cbs = E(nc.sbuf_tensor("cbs", [128, 8], F32))
        qbs = E(nc.sbuf_tensor("qbs", [128, 8], F32))
        qbs2 = E(nc.sbuf_tensor("qbs2", [128, 8], F32))
        obs = E(nc.sbuf_tensor("obs", [128, 8], F32))
        rts = E(nc.sbuf_tensor("rts", [128, CC, 8], F32))
        bvr_b = E(nc.sbuf_tensor("bvr_b", [1, C], BF16))
        ones_row = E(nc.sbuf_tensor("ones_row", [1, 512], F32))
        ones_row_b = E(nc.sbuf_tensor("ones_row_b", [1, 512], BF16))
        ones_col = E(nc.sbuf_tensor("ones_col", [128, 1], F32))
        eps1 = E(nc.sbuf_tensor("eps1", [1, 1], F32))
        iota_p = E(nc.sbuf_tensor("iota_p", [128, 1], F32))
        idn32 = E(nc.sbuf_tensor("idn32", [128, 128], F32))
        r48 = E(nc.sbuf_tensor("r48", [48, 128], F32))
        bv_off = []
        bv_tot = 0
        for h in range(NH):
            bv_off.append(bv_tot)
            bv_tot += max(WT[h], 3)
        GOFF = [0, 128, 384, 768]
        st_mean = E(nc.sbuf_tensor("st_mean", [1, 512], F32))
        st_msq = E(nc.sbuf_tensor("st_msq", [1, 512], F32))
        st_rstd = E(nc.sbuf_tensor("st_rstd", [1, 512], F32))
        sgbuf = E(nc.sbuf_tensor("sgbuf", [128, 2, 512], BF16))
        tmpf = E(nc.sbuf_tensor("tmpf", [128, 2, 512], F32))
        eqb = E(nc.sbuf_tensor("eqb", [128, 8], F32))
        remb = E(nc.sbuf_tensor("remb", [128, 1], F32))
        rmaxb = E(nc.sbuf_tensor("rmaxb", [128, 1], F32))
        mtm = E(nc.sbuf_tensor("mtm", [128, 8], F32))
        z8 = E(nc.sbuf_tensor("z8", [128, 8], F32))

        PB = [E(nc.psum_tensor(f"pb{i}", [128, 512], F32)) for i in range(8)]
        bank_free = {i: None for i in range(8)}

        def te_claim(b):
            if bank_free[b] is not None:
                which, n = bank_free[b]
                if n > 0:
                    TE.wait_ge(sems[which][0], n)
                bank_free[b] = None

        def release(b, which):
            bank_free[b] = (which, cnt(which))

        guard = {}

        def gwait(eng, key):
            if key in guard:
                which, n = guard[key]
                if n > 0:
                    eng.wait_ge(sems[which][0], n)
                del guard[key]

        def gset(key, which):
            guard[key] = (which, cnt(which))

        def vsync(instr):
            inc("ve", instr, 1)
            wait(VE, "ve")
            return instr

        CA1, NCA1, NSACA1, CB1, NCB1, NSBCB1, GA, GB, CGA = 0, 8, 16, 24, 32, 40, 48, 56, 64

        es_rb = contextlib.ExitStack()
        Erb = es_rb.enter_context
        ciota = Erb(nc.sbuf_tensor("ciota", [128, 512], F32, side="right"))
        Dm = [Erb(nc.sbuf_tensor(f"Dm_{c4}", [128, 128 * (c4 + 1)], BF16, side="right")) for c4 in range(4)]
        bv_all = Erb(nc.sbuf_tensor("bv_all", [128, bv_tot], F32, side="right"))
        valid_s = Erb(nc.sbuf_tensor("valid_s", [128, 48], F32, side="right"))
        es_rb2 = contextlib.ExitStack()
        x2buf = es_rb2.enter_context(nc.sbuf_tensor("x2buf", [128, 2, 512], F32, side="right"))
        tbuf = es_rb2.enter_context(nc.sbuf_tensor("tbuf", [128, 2, 512], F32, side="right"))

        inc("sy", SY.dma_start(out=valid_s[:], in_=dvalid), 16)
        inc("sy", SY.dma_start(out=cbs[:], in_=dcb), 16)
        inc("sy", SY.dma_start(out=qbs[:], in_=dqb), 16)
        inc("sy", SY.dma_start(out=obs[:], in_=dob), 16)
        inc("sy", SY.dma_start(out=rts[:], in_=drt.rearrange("(c p) e -> p c e", c=CC, p=128)), 16)
        inc("gd", GE.dma_start(out=bvr_b[:], in_=dbvr), 16)
        n_small_loads = cnt("sy")

        GE.memset(z8[:], 0.0)
        GE.memset(ones_row[:], 1.0)
        GE.memset(ones_row_b[:], 1.0)
        GE.memset(ones_col[:], 1.0)
        GE.memset(eps1[:], 1e-5)
        GE.iota(iota_p[:], pattern=[[0, 1]], base=0, channel_multiplier=1,
                allow_small_or_imprecise_dtypes=True)
        GE.iota(ciota[:], pattern=[[1, 512]], base=0, channel_multiplier=0,
                allow_small_or_imprecise_dtypes=True)
        GE.memset(idn32[:], 1.0)
        GE.affine_select(idn32[:], idn32[:], pattern=[[-1, 128]], compare_op=ALU.is_equal,
                         fill=0.0, channel_multiplier=1)
        for c4 in range(4):
            GE.iota(Dm[c4][:], pattern=[[128, c4 + 1], [-1, 128]], base=-128 * c4,
                    channel_multiplier=1, allow_small_or_imprecise_dtypes=True)
            ins = GE.affine_select(Dm[c4][:], Dm[c4][:], pattern=[[-128, c4 + 1], [1, 128]],
                                   base=128 * c4, compare_op=ALU.is_ge, fill=NEG,
                                   channel_multiplier=-1)
        inc("gc", ins, 1)

        wait(VE, "sy", n_small_loads)
        wait(VE, "gc")
        VE.tensor_scalar(qbs2[:], qbs[:], 0.125, 0.0, ALU.mult, ALU.add)
        for h in range(NH):
            for d in range(1, max(WT[h], 3) + 1):
                VE.tensor_scalar(bv_all[:, bv_off[h] + d - 1: bv_off[h] + d], iota_p[:],
                                 SLOPES[h], -128.0 * d * SLOPES[h], ALU.mult, ALU.add)
        wait(SE, "sy", n_small_loads)
        inc("ac", SE.activation(out=silu_c[:], in_=cbs[:], func=AF.Silu), 1)

        es_r3 = contextlib.ExitStack()
        XBUF = [es_r3.enter_context(nc.sbuf_tensor(f"xbuf{i}", [128, CC, 512], F32, side="right")) for i in range(2)]

        for j in range(12):
            b = j % 2
            xsem = "xa" if b == 0 else "xb"
            gwait(SY, f"xbuf{b}")
            inc(xsem, SY.dma_start(out=XBUF[b][:], in_=dwm[:, j]), 16)
            wait(TE, xsem)
            wait(TE, "ac")
            te_claim(4)
            for cc in range(CC):
                ins = TE.matmul(PB[4][0:1, :], silu_c[:, cc:cc + 1], XBUF[b][:, cc, :],
                                start=(cc == 0), stop=(cc == CC - 1))
            inc("pe", ins, 1)
            gset(f"xbuf{b}", "pe")
            wait(SE, "pe")
            gwait(SE, "modrow1")
            inc("ac", SE.copy(out=st_mean[:], in_=PB[4][0:1, :]), 1)
            release(4, "ac")
            wait(GE, "ac")
            inc("gd", GE.dma_start(out=dscr[512 * j: 512 * (j + 1)].rearrange("(o a) -> o a", o=1),
                                   in_=st_mean[:]), 16)
            gset("modrow1", "gd")
        wait(GE, "gd")
        inc("gd", GE.dma_start(out=r48[:], in_=dscr.rearrange("(a b) -> a b", a=48, b=128)), 16)
        wait(TE, "gd")
        wait(TE, "gc")
        te_claim(5)
        inc("pe", TE.transpose(PB[5][:, 0:48], r48[:], idn32[0:48, 0:48]), 1)
        wait(SE, "pe")
        inc("ac", SE.copy(out=modv[:], in_=PB[5][:, 0:48]), 1)
        release(5, "ac")

        wait(VE, "ac")
        wait(VE, "sy", n_small_loads)
        vsync(VE.tensor_scalar(modd[:, CA1:CA1 + 8], modv[:, 8:16], 1.0, 1.0, ALU.mult, ALU.add))
        vsync(VE.tensor_scalar(modd[:, NCA1:NCA1 + 8], modd[:, CA1:CA1 + 8], -1.0, 0.0, ALU.mult, ALU.add))
        vsync(VE.tensor_scalar(modd[:, CB1:CB1 + 8], modv[:, 32:40], 1.0, 1.0, ALU.mult, ALU.add))
        vsync(VE.tensor_scalar(modd[:, NCB1:NCB1 + 8], modd[:, CB1:CB1 + 8], -1.0, 0.0, ALU.mult, ALU.add))
        for (src, dst) in ((16, GA), (40, GB)):
            vsync(VE.tensor_scalar(modd[:, dst:dst + 8], modv[:, src:src + 8], -1.0, 0.0, ALU.mult, ALU.add))
            vsync(VE.tensor_tensor(modd[:, dst:dst + 8], modd[:, dst:dst + 8], modv[:, src:src + 8], ALU.max))
            vsync(VE.tensor_scalar(modd[:, dst:dst + 8], modd[:, dst:dst + 8], 0.25, 0.0, ALU.max, ALU.add))
            vsync(VE.tensor_tensor(eqb[:], modv[:, src:src + 8], z8[:], ALU.is_ge))
            vsync(VE.tensor_scalar(mtm[:], eqb[:], 2.0, -1.0, ALU.mult, ALU.add))
            vsync(VE.tensor_tensor(modd[:, dst:dst + 8], modd[:, dst:dst + 8], mtm[:], ALU.mult))
        vsync(VE.tensor_tensor(modd[:, CGA:CGA + 8], modd[:, GA:GA + 8], obs[:], ALU.mult))

        if stage == "mod":
            wait(SY, "ve")
            inc("sy", SY.dma_start(out=dbg["modv"], in_=modv[:]), 16)
            inc("sy", SY.dma_start(out=dbg["modd"], in_=modd[:]), 16)
            inc("sy", SY.dma_start(out=dbg["eqb"], in_=eqb[:]), 16)
            inc("sy", SY.dma_start(out=dbg["mtm"], in_=mtm[:]), 16)
            wait(GE, "sy")
            es_r3.close()
            return nc

        def emit_ln_h(src_tile, dst_fn, ca1_col, sh_col, h2f32_dst=None, post_cc=None):
            te_claim(0)
            te_claim(1)
            for cc in range(CC):
                b = cc % 2
                gwait(SE, f"x2_{b}")
                inc("ac", SE.activation(out=x2buf[:, b, :], in_=src_tile(cc), func=AF.Square), 1)
                ins = TE.matmul(PB[0][0:1, :], ones_col[:], src_tile(cc),
                                start=(cc == 0), stop=(cc == CC - 1))
                if cc == CC - 1:
                    inc("pe", ins, 1)
                wait(TE, "ac")
                ins = TE.matmul(PB[1][0:1, :], ones_col[:], x2buf[:, b, :],
                                start=(cc == 0), stop=(cc == CC - 1))
                inc("pe", ins, 1)
                gset(f"x2_{b}", "pe")
            wait(VE, "pe")
            vsync(VE.tensor_scalar(st_mean[:], PB[0][0:1, :], 1.0 / C, 0.0, ALU.mult, ALU.add))
            vsync(VE.tensor_tensor(st_msq[:], st_mean[:], st_mean[:], ALU.mult))
            ins = VE.scalar_tensor_tensor(st_msq[:], PB[1][0:1, :], 1.0 / C, st_msq[:],
                                          ALU.mult, ALU.subtract)
            inc("ve", ins, 1)
            release(0, "ve")
            release(1, "ve")
            wait(SE, "ve")
            SE.activation(out=st_msq[:], in_=st_msq[:], func=AF.Ln, bias=eps1[:])
            ins = SE.activation(out=st_rstd[:], in_=st_msq[:], func=AF.Exp, scale=-0.5)
            inc("ac", ins, 1)
            wait(TE, "ac")
            te_claim(2)
            te_claim(3)
            inc("pe", TE.matmul(PB[2][:], ones_row[:, 0:128], st_rstd[:], start=True, stop=True), 1)
            inc("pe", TE.matmul(PB[3][:], ones_row[:, 0:128], st_mean[:], start=True, stop=True), 1)
            for cc in range(CC):
                wait(VE, "pe")
                b = cc % 2
                VE.tensor_tensor(tbuf[:, b, :], src_tile(cc), PB[3][:], ALU.subtract)
                VE.scalar_tensor_tensor(tbuf[:, 1 - b, :], tbuf[:, b, :],
                                        modd[:, ca1_col + cc:ca1_col + cc + 1],
                                        PB[2][:], ALU.mult, ALU.mult)
                if h2f32_dst is not None:
                    VE.tensor_scalar(h2f32_dst(cc), tbuf[:, 1 - b, :],
                                     modv[:, sh_col + cc:sh_col + cc + 1], 0.0, ALU.add, ALU.add)
                    ins = VE.tensor_copy(dst_fn(cc), h2f32_dst(cc))
                    inc("ve", ins, 1)
                else:
                    ins = VE.tensor_scalar(dst_fn(cc), tbuf[:, 1 - b, :],
                                           modv[:, sh_col + cc:sh_col + cc + 1], 0.0, ALU.add, ALU.add)
                    inc("ve", ins, 1)
                if post_cc is not None:
                    post_cc(cc)
            release(2, "ve")
            release(3, "ve")

        es_r2 = contextlib.ExitStack()
        h1r = es_r2.enter_context(nc.sbuf_tensor("h1r", [128, CC, TL - PAD], BF16))

        def h1_slice(cc, u0, w):
            if u0 < PAD:
                return h2t[:, cc, u0:u0 + w]
            return h1r[:, cc, u0 - PAD:u0 - PAD + w]

        for tt in range(3, TT_N):
            b = tt % 2
            u0 = tt * 512
            xsem = "xa" if b == 0 else "xb"
            gwait(SY, f"xbuf{b}")
            inc(xsem, SY.dma_start(out=XBUF[b][:], in_=dx[:, tt]), 16)
            wait(TE, xsem)
            wait(SE, xsem)
            wait(VE, xsem)

            def src_tile(cc, _b=b):
                return XBUF[_b][:, cc, :]

            def dst_fn(cc, _u0=u0):
                return h1_slice(cc, _u0, 512)

            emit_ln_h(src_tile, dst_fn, CA1, 0)
            gset(f"xbuf{b}", "ve")

        if stage == "h1":
            wait(SY, "ve")
            inc("sy", SY.dma_start(out=dbg["h1"], in_=h1r[:]), 16)
            inc("sy", SY.dma_start(out=dbg["h1p"], in_=h2t[:]), 16)
            wait(GE, "sy")
            es_r3.close(); es_r2.close()
            return nc

        es_r3.close()
        es_rb2.close()
        es_r4 = contextlib.ExitStack()
        y_all = es_r4.enter_context(nc.sbuf_tensor("y_all", [128, CC, OWNT], BF16, side="right"))
        k2 = es_r3.enter_context(nc.sbuf_tensor("k2", [128, TT_N * 512], BF16, side="right"))
        q2 = es_r3.enter_context(nc.sbuf_tensor("q2", [128, OWN, 512], BF16, side="right"))
        v2 = es_r3.enter_context(nc.sbuf_tensor("v2", [128, KT_MAX, 130], BF16, side="right"))
        Gcur = es_r3.enter_context(nc.sbuf_tensor("Gcur", [128, 1280], BF16, side="right"))
        colf = es_r3.enter_context(nc.sbuf_tensor("colf", [128, 2, 512], BF16, side="right"))
        qwt = es_r3.enter_context(nc.sbuf_tensor("qwt", [128, 2, CC, 128], BF16, side="right"))
        kvwt = es_r3.enter_context(nc.sbuf_tensor("kvwt", [128, 2, CC, 256], BF16, side="right"))
        e1buf = es_r3.enter_context(nc.sbuf_tensor("e1buf", [128, 3, 512], BF16, side="right"))
        pbuf = es_r3.enter_context(nc.sbuf_tensor("pbuf", [128, 3, 512], BF16, side="right"))
        recip_rb = es_r3.enter_context(nc.sbuf_tensor("recip_rb", [1, 512], BF16, side="right"))
        srow = st_mean
        recip_r = st_msq
        srow_b = sgbuf[:, 1, :]
        recips = sgbuf[:, 0, :]

        wait(GE, "ve")
        wait(GE, "pe")
        inc("gc", GE.memset(v2[:], 1.0), 1)
        inc("gc", GE.memset(k2[:, 0:1536], 0.0), 1)

        norm_pending = [None]
        yb_idx = [0]

        for hp in range(HPAIRS):
            if norm_pending[0] is not None:
                norm_pending[0][0]()
                norm_pending[0][1]()
                norm_pending[0] = None
            wslot = hp % 2
            psem = "pw0" if wslot == 0 else "pw1"
            if hp == 0:
                gwait(GE, f"pairw_{wslot}")
                inc(psem, GE.dma_start(out=qwt[:, wslot], in_=dqw[hp]), 16)
                inc(psem, GE.dma_start(out=kvwt[:, wslot], in_=dkvw[hp]), 16)
            wait(TE, psem)
            wait(TE, "gc")

            for tt in range(3, TT_N):
                u0 = tt * 512
                kb = 4 + tt % 2
                te_claim(kb)
                for cc in range(CC):
                    ins = TE.matmul(PB[kb][:], kvwt[:, wslot, cc, 0:128], h1_slice(cc, u0, 512),
                                    start=(cc == 0), stop=(cc == CC - 1))
                inc("pe", ins, 1)
                wait(SE, "pe")
                inc("ac", SE.copy(out=k2[:, u0:u0 + 512], in_=PB[kb][:]), 1)
                release(kb, "ac")
            for qi in range(OWN):
                off = qi * 1024
                kb = 4 + qi % 2
                te_claim(kb)
                for cc in range(CC):
                    ins = TE.matmul(PB[kb][:], qwt[:, wslot, cc, :], h1r[:, cc, off:off + 512],
                                    start=(cc == 0), stop=(cc == CC - 1))
                inc("pe", ins, 1)
                wait(SE, "pe")
                inc("ac", SE.activation(out=q2[:, qi, :], in_=PB[kb][:], func=AF.Identity,
                                        scale=0.125, bias=qbs2[:, hp:hp + 1]), 1)
                release(kb, "ac")
            for kt in range(12, KT_MAX):
                u0 = kt * 128
                vb = 3 if kt % 2 == 0 else 6
                te_claim(vb)
                for cc in range(CC):
                    ins = TE.matmul(PB[vb][:, 0:128], h1_slice(cc, u0, 128), kvwt[:, wslot, cc, 128:256],
                                    start=(cc == 0), stop=(cc == CC - 1))
                inc("pe", ins, 1)
                wait(VE, "pe")
                VE.tensor_copy(v2[:, kt, 0:64], PB[vb][:, 0:64])
                ins = VE.tensor_copy(v2[:, kt, 65:129], PB[vb][:, 64:128])
                inc("ve", ins, 1)
                release(vb, "ve")
            if hp + 1 < HPAIRS:
                ns = (hp + 1) % 2
                nsem = "pw0" if ns == 0 else "pw1"
                gwait(GE, f"pairw_{ns}")
                inc(nsem, GE.dma_start(out=qwt[:, ns], in_=dqw[hp + 1]), 16)
                inc(nsem, GE.dma_start(out=kvwt[:, ns], in_=dkvw[hp + 1]), 16)
            gset(f"pairw_{wslot}", "pe")

            wait(SE, "ve")
            for hip in range(2):
                ins = SE.activation(out=colf[:, hip, :], in_=ciota[:], func=AF.Exp,
                                    scale=-SLOPES[2 * hp + hip])
            inc("ac", ins, 1)
            wait(TE, "ac")

            for hip in range(2):
                h = 2 * hp + hip
                r0 = 64 * hip
                vlo, vhi = (0, 65) if hip == 0 else (65, 130)
                gwait(SE, "gcur")
                for c4 in range(4):
                    ins = SE.activation(out=Gcur[:, GOFF[c4]:GOFF[c4] + 128 * (c4 + 1)],
                                        in_=Dm[c4][:], func=AF.Exp, scale=SLOPES[h])
                inc("ac", ins, 1)
                for qi in range(OWN):
                    q0t = (PAD + qi * 1024) // 128
                    yb = 2 + (yb_idx[0] % 2)
                    yb_idx[0] += 1
                    slots = [("off", q0t - dlt, dlt) for dlt in range(WT[h], 0, -1)
                             if q0t - dlt >= 12]
                    slots += [("diag", c4) for c4 in range(4)]
                    n_s = len(slots)
                    av_ready = {}

                    SBMAP = (0, 1, 4)

                    def emit_front(i):
                        kind = slots[i][0]
                        r = i % 3
                        sb = SBMAP[r]
                        te_claim(sb)
                        if kind == "off":
                            _, kt, dlt = slots[i]
                            s_rdy = inc("pe", TE.matmul(PB[sb][:], k2[r0:r0 + 64, kt * 128:kt * 128 + 128],
                                                        q2[r0:r0 + 64, qi, :], start=True, stop=True), 1)
                            wait(SE, "pe", s_rdy)
                            gwait(SE, f"e1_{r}")
                            inc("ac", SE.activation(out=e1buf[:, r, :], in_=PB[sb][:], func=AF.Exp,
                                                    bias=bv_all[:, bv_off[h] + dlt - 1:bv_off[h] + dlt]), 1)
                            release(sb, "ac")
                            gwait(VE, f"pb_{r}")
                            wait(VE, "ac")
                            ins = VE.scalar_tensor_tensor(pbuf[:, r, :], e1buf[:, r, :],
                                                          valid_s[:, kt:kt + 1], colf[:, hip, :],
                                                          ALU.mult, ALU.mult)
                            inc("ve", ins, 1)
                            gset(f"e1_{r}", "ve")
                        else:
                            _, c4 = slots[i]
                            w = 128 * (c4 + 1)
                            for j4 in range(c4 + 1):
                                kt = q0t + j4
                                ins = TE.matmul(PB[sb][:, 128 * j4:128 * j4 + 128],
                                                k2[r0:r0 + 64, kt * 128:kt * 128 + 128],
                                                q2[r0:r0 + 64, qi, 128 * c4:128 * c4 + 128],
                                                start=True, stop=True)
                            s_rdy = inc("pe", ins, 1)
                            wait(SE, "pe", s_rdy)
                            gwait(SE, f"e1_{r}")
                            inc("ac", SE.activation(out=e1buf[:, r, 0:w], in_=PB[sb][:, 0:w],
                                                    func=AF.Exp), 1)
                            release(sb, "ac")
                            gwait(VE, f"pb_{r}")
                            wait(VE, "ac")
                            ins = VE.tensor_tensor(pbuf[:, r, 0:w], e1buf[:, r, 0:w],
                                                   Gcur[:, GOFF[c4]:GOFF[c4] + w], ALU.mult)
                            inc("ve", ins, 1)
                            gset(f"e1_{r}", "ve")
                        av_ready[i] = cnt("ve")

                    def emit_av(i, _yb=yb, _q0t=q0t):
                        kind = slots[i][0]
                        r = i % 3
                        wait(TE, "ve", av_ready[i])
                        if kind == "off":
                            _, kt, dlt = slots[i]
                            inc("pe", TE.matmul(PB[_yb][0:65, :], v2[:, kt, vlo:vhi],
                                                pbuf[:, r, :], start=(i == 0), stop=False), 1)
                        else:
                            _, c4 = slots[i]
                            for j4 in range(c4 + 1):
                                kt = _q0t + j4
                                ins = TE.matmul(PB[_yb][0:65, 128 * c4:128 * c4 + 128],
                                                v2[:, kt, vlo:vhi],
                                                pbuf[:, r, 128 * j4:128 * j4 + 128],
                                                start=False, stop=False)
                            inc("pe", ins, 1)
                        gset(f"pb_{r}", "pe")

                    te_claim(yb)
                    for i in range(n_s + 1):
                        if i < n_s:
                            emit_front(i)
                        if i == 1 and norm_pending[0] is not None:
                            norm_pending[0][0]()
                        if i == min(5, n_s) and norm_pending[0] is not None:
                            norm_pending[0][1]()
                            norm_pending[0] = None
                        if i >= 1:
                            emit_av(i - 1)
                    av_last = cnt("pe")

                    def make_norm(_yb=yb, _h=h, _qi=qi, _av=None):
                        st = {}

                        def norm_dve():
                            wait(VE, "pe", _av)
                            vsync(VE.tensor_copy(srow[:], PB[_yb][64:65, :]))
                            VE.tensor_copy(srow_b[0:1, :], srow[:])
                            vsync(VE.reciprocal(recip_r[:], srow[:]))
                            ins = VE.tensor_copy(recip_rb[:], recip_r[:])
                            inc("ve", ins, 1)
                            st["dve"] = cnt("ve")

                        def norm_te():
                            wait(TE, "ve", st["dve"])
                            inc("pe", TE.matmul(PB[_yb][0:64, :], bvr_b[:, 64 * _h:64 * _h + 64],
                                                srow_b[0:1, :], start=False, stop=True), 1)
                            te_claim(7)
                            inc("pe", TE.matmul(PB[7][0:64, :], ones_row_b[:, 0:64], recip_rb[:],
                                                start=True, stop=True), 1)
                            wait(VE, "pe")
                            VE.tensor_copy(recips[0:64, :], PB[7][0:64, :])
                            ccv = _h // 2
                            rr = 64 * (_h % 2)
                            ins = VE.tensor_tensor(y_all[rr:rr + 64, ccv, _qi * 512:_qi * 512 + 512],
                                                   PB[_yb][0:64, :], recips[0:64, :], ALU.mult)
                            inc("ve", ins, 1)
                            release(_yb, "ve")
                            release(7, "ve")
                        return (norm_dve, norm_te)

                    norm_pending[0] = make_norm(_av=av_last)
                gset("gcur", "ve")
        if norm_pending[0] is not None:
            norm_pending[0][0]()
            norm_pending[0][1]()
            norm_pending[0] = None

        if stage == "attn":
            wait(SY, "ve")
            inc("sy", SY.dma_start(out=dbg["yall"], in_=y_all[:]), 16)
            wait(GE, "sy")
            es_r3.close(); es_r4.close(); es_r2.close()
            return nc

        es_r3.close()
        x2buf = es_r3.enter_context(nc.sbuf_tensor("x2buf_p3", [128, 2, 512], F32, side="right"))
        tbuf = es_r3.enter_context(nc.sbuf_tensor("tbuf_p3", [128, 2, 512], F32, side="right"))
        owt = es_r3.enter_context(nc.sbuf_tensor("owt", [128, CC, C], BF16, side="right"))
        xq2 = es_r3.enter_context(nc.sbuf_tensor("xq2", [128, 2, 512], F32, side="right"))
        h2f_c = es_r3.enter_context(nc.sbuf_tensor("h2f_c", [128, 512], F32, side="right"))
        logits_sb = es_r3.enter_context(nc.sbuf_tensor("logits_sb", [128, 32], F32, side="right"))
        es_r2.close()
        yacc = es_r2.enter_context(nc.sbuf_tensor("yacc", [128, CC, OWNT], F32))
        m8buf = es_r2.enter_context(nc.sbuf_tensor("m8buf", [8, 128], F32))
        wait(GE, "pe")
        wait(GE, "ve")
        inc("gd", GE.dma_start(out=owt[:], in_=dow), 16)

        lgdump = None
        if stage == "fdbg":
            lgdump = es_r2.enter_context(nc.sbuf_tensor("lgdump", [128, 4, 32], F32))
        first_p3 = True
        for qi in range(OWN):
            u0 = PAD + qi * 1024
            if first_p3:
                wait(SY, "pe")
                wait(SY, "ve")
                wait(TE, "ve")
                first_p3 = False
            for cco in range(CC):
                ob_ = cco % 2
                gwait(SY, f"xq_{ob_}")
                inc("xq", SY.dma_start(out=xq2[:, ob_, :], in_=dx[:, u0 // 512, cco]), 16)
                wait(TE, "gd")
                te_claim(6 + ob_)
                for cci in range(CC):
                    ins = TE.matmul(PB[6 + ob_][:], owt[:, cci, 128 * cco:128 * cco + 128],
                                    y_all[:, cci, qi * 512:qi * 512 + 512],
                                    start=(cci == 0), stop=(cci == CC - 1))
                inc("pe", ins, 1)
                wait(VE, "pe")
                wait(VE, "xq")
                VE.scalar_tensor_tensor(tmpf[:, ob_, :], PB[6 + ob_][:], modd[:, GA + cco:GA + cco + 1],
                                        xq2[:, ob_, :], ALU.mult, ALU.add)
                ins = VE.tensor_scalar(yacc[:, cco, qi * 512:qi * 512 + 512], tmpf[:, ob_, :],
                                       modd[:, CGA + cco:CGA + cco + 1], 0.0, ALU.add, ALU.add)
                inc("ve", ins, 1)
                release(6 + ob_, "ve")
                gset(f"xq_{ob_}", "ve")

            def src2(cc, _qi=qi):
                return yacc[:, cc, _qi * 512:_qi * 512 + 512]

            def dst2(cc, _qi=qi):
                return h2t[:, cc, _qi * 512:_qi * 512 + 512]

            def dsth2f(cc):
                return h2f_c[:]

            def post_cc(cc):
                te_claim(5)
                wait(TE, "ve")
                for sub in range(4):
                    ins = TE.matmul(PB[5][:, 8 * sub:8 * sub + 8],
                                    h2f_c[:, 128 * sub:128 * sub + 128], rts[:, cc, :],
                                    start=True, stop=True)
                inc("pe", ins, 1)
                wait(VE, "pe")
                if cc == 0:
                    vsync(VE.tensor_copy(logits_sb[:], PB[5][:, 0:32]))
                else:
                    vsync(VE.tensor_tensor(logits_sb[:], logits_sb[:], PB[5][:, 0:32], ALU.add))
                release(5, "ve")

            wait(TE, "ve")
            wait(SE, "ve")
            emit_ln_h(src2, dst2, CB1, 24, h2f32_dst=dsth2f, post_cc=post_cc)

            if stage == "fdbg":
                ins = VE.tensor_copy(lgdump[:, qi, :], logits_sb[:])
                inc("ve", ins, 1)
            for sub in range(4):
                wait(VE, "pe")
                vsync(VE.reduce_max(rmaxb[:], logits_sb[:, 8 * sub:8 * sub + 8], axis=AXX))
                vsync(VE.tensor_scalar(mtm[:], logits_sb[:, 8 * sub:8 * sub + 8], 0.0, rmaxb[:],
                                       ALU.mult, ALU.add))
                vsync(VE.tensor_tensor(eqb[:], logits_sb[:, 8 * sub:8 * sub + 8], mtm[:], ALU.is_ge))
                vsync(VE.memset(remb[:], 1.0))
                for e_ in range(8):
                    vsync(VE.tensor_tensor(mtm[:, e_:e_ + 1], eqb[:, e_:e_ + 1], remb[:], ALU.mult))
                    vsync(VE.scalar_tensor_tensor(remb[:], mtm[:, e_:e_ + 1], -1.0, remb[:],
                                                  ALU.mult, ALU.add))
                wait(TE, "ve")
                te_claim(4)
                inc("pe", TE.transpose(PB[4][0:8, 0:128], mtm[:], idn32[:]), 1)
                wait(VE, "pe")
                gwait(VE, "m8")
                ins = VE.tensor_copy(m8buf[:], PB[4][0:8, 0:128])
                inc("ve", ins, 1)
                release(4, "ve")
                wait(SY, "ve")
                inc("dk", SY.dma_start(out=dmask[:, qi * 512 + 128 * sub: qi * 512 + 128 * sub + 128],
                                       in_=m8buf[:]), 16)
                gset("m8", "dk")
            release(5, "ve")

        if stage == "x1":
            wait(SY, "ve")
            inc("sy", SY.dma_start(out=dbg["x1"], in_=yacc[:]), 16)
            inc("sy", SY.dma_start(out=dbg["h2"], in_=h2t[:]), 16)
            wait(GE, "sy")
            mrall = es_r3.enter_context(nc.sbuf_tensor("mrall", [8, OWNT], BF16, side="right"))
            inc("gd", GE.dma_start(out=mrall[:], in_=dmask), 16)
            wait(SY, "gd")
            inc("sy", SY.dma_start(out=dbg["mrows"], in_=mrall[:]), 16)
            wait(GE, "sy")
            es_r3.close(); es_r4.close(); es_r2.close()
            return nc

        if stage == "fdbg":
            wait(SY, "ve")
            inc("sy", SY.dma_start(out=dbg["lg"], in_=lgdump[:]), 16)
            wait(GE, "sy")
            mrall2 = es_r3.enter_context(nc.sbuf_tensor("mrall2", [8, OWNT], F32, side="right"))
            inc("gd", GE.dma_start(out=mrall2[:], in_=dmask), 16)
            wait(SY, "gd")
            inc("sy", SY.dma_start(out=dbg["dmask"], in_=mrall2[:]), 16)
            wait(GE, "sy")
            es_r3.close(); es_r4.close(); es_rb.close(); es_r2.close()
            return nc

        es_r3.close()
        es_r4.close()
        es_rb.close()
        dn_s = es_r3.enter_context(nc.sbuf_tensor("dn_s", [128, HHN, C], BF16, side="right"))
        act_b = es_r3.enter_context(nc.sbuf_tensor("act_b", [128, HHN, 1024], BF16, side="right"))
        ugb = es_r3.enter_context(nc.sbuf_tensor("ugb", [128, 4, 2, CC, 128], BF16, side="right"))
        maskb = es_r3.enter_context(nc.sbuf_tensor("maskb", [128, OWNT], BF16, side="right"))
        mrow1 = es_r3.enter_context(nc.sbuf_tensor("mrow1", [1, 2, 512], BF16, side="right"))

        wait(GE, "pe")
        wait(GE, "ve")
        wait(GE, "dk")

        def emit_mask(et):
            for sg in range(4):
                mb = sg % 2
                msem = "mr" if mb == 0 else "m1"
                gwait(GE, f"mrow_{mb}")
                inc(msem, GE.dma_start(out=mrow1[:, mb, :],
                                       in_=dmask[et:et + 1, 512 * sg:512 * sg + 512]), 16)
                wait(TE, msem)
                te_claim(6)
                inc("pe", TE.matmul(PB[6][:], ones_row_b[:, 0:128], mrow1[0:1, mb, :],
                                    start=True, stop=True), 1)
                gset(f"mrow_{mb}", "pe")
                wait(VE, "pe")
                ins = VE.tensor_copy(maskb[:, 512 * sg:512 * sg + 512], PB[6][:])
                inc("ve", ins, 1)
                release(6, "ve")

        emit_mask(0)
        ug_ready = {}
        dn_guard = 0
        for et in range(NE):
            dn_chunks = list(ddnt[et, :, 4 * ci:4 * ci + 4, :] for ci in range(4))
            for hfp in range(2):
                t0 = hfp * 1024
                for hh in range(HHN):
                    al = hh % 4
                    usem = ("u0", "u1", "u2", "u3")[al]
                    gwait(GE, f"ug_{al}")
                    inc(usem, GE.dma_start(out=ugb[:, al, 0, :, :], in_=dupt[et, hh]), 16)
                    inc(usem, GE.dma_start(out=ugb[:, al, 1, :, :], in_=dgat[et, hh]), 16)
                    if hfp == 0 and hh % 4 == 1:
                        ci = hh // 4
                        if dn_guard > 0:
                            wait(GE, "pe", dn_guard)
                        inc("dn", GE.dma_start(out=dn_s[:, 4 * ci:4 * ci + 4, :],
                                               in_=dn_chunks[ci]), 16)
                    for tf in range(2):
                        al2 = tf % 2
                        wait(TE, usem)
                        te_claim(0 + al2)
                        for cc in range(CC):
                            ins = TE.matmul(PB[0 + al2][:], ugb[:, al, 0, cc, :],
                                            h2t[:, cc, t0 + 512 * tf:t0 + 512 * tf + 512],
                                            start=(cc == 0), stop=(cc == CC - 1))
                        inc("pe", ins, 1)
                        te_claim(2 + al2)
                        for cc in range(CC):
                            ins = TE.matmul(PB[2 + al2][:], ugb[:, al, 1, cc, :],
                                            h2t[:, cc, t0 + 512 * tf:t0 + 512 * tf + 512],
                                            start=(cc == 0), stop=(cc == CC - 1))
                        inc("pe", ins, 1)
                        if tf == 1:
                            gset(f"ug_{al}", "pe")
                        wait(SE, "pe")
                        gwait(SE, f"sg_{al2}")
                        inc("ac", SE.activation(out=sgbuf[:, al2, :], in_=PB[2 + al2][:],
                                                func=AF.Silu), 1)
                        release(2 + al2, "ac")
                        wait(VE, "ac")
                        wait(VE, "pe")
                        ins = VE.tensor_tensor(act_b[:, hh, 512 * tf:512 * tf + 512],
                                               sgbuf[:, al2, :], PB[0 + al2][:], ALU.mult)
                        inc("ve", ins, 1)
                        gset(f"sg_{al2}", "ve")
                        release(0 + al2, "ve")
                if hfp == 0 and et + 1 < EXP + 1 and et + 1 < EXP:
                    pass
                wait(TE, "ve")
                wait(TE, "dn")
                for tf in range(2):
                    for cc in range(CC):
                        db = 4 + (cc + 2 * tf) % 2
                        te_claim(db)
                        for hh in range(HHN):
                            ins = TE.matmul(PB[db][:], dn_s[:, hh, 128 * cc:128 * cc + 128],
                                            act_b[:, hh, 512 * tf:512 * tf + 512],
                                            start=(hh == 0), stop=(hh == HHN - 1))
                        inc("pe", ins, 1)
                        wait(VE, "pe")
                        tt0 = t0 + 512 * tf
                        yslice = yacc[:, cc, tt0:tt0 + 512]
                        if et < EXP:
                            VE.tensor_tensor(tmpf[:, cc % 2, :], PB[db][:],
                                             maskb[:, tt0:tt0 + 512], ALU.mult)
                            ins = VE.scalar_tensor_tensor(yslice, tmpf[:, cc % 2, :],
                                                          modd[:, GB + cc:GB + cc + 1], yslice,
                                                          ALU.mult, ALU.add)
                        else:
                            ins = VE.scalar_tensor_tensor(yslice, PB[db][:],
                                                          modd[:, GB + cc:GB + cc + 1], yslice,
                                                          ALU.mult, ALU.add)
                        inc("ve", ins, 1)
                        release(db, "ve")
                if hfp == 1 and et + 1 < EXP:
                    emit_mask(et + 1)
            dn_guard = cnt("pe")

        wait(SY, "ve")
        inc("sy", SY.dma_start(out=dout.rearrange("(c p) t -> p c t", c=CC, p=128), in_=yacc[:]), 16)
        wait(GE, "sy")
        es_r3.close(); es_r2.close()
    return nc


def _stage_inputs(L0_attn_x, L0_attn_q_w, L0_attn_q_b, L0_attn_kv_w, L0_attn_kv_b,
                  L0_attn_o_w, L0_attn_o_b, L0_ffn_up, L0_ffn_gate, L0_ffn_down,
                  L0_ffn_router, L0_ffn_w_up_s, L0_ffn_w_gate_s, L0_ffn_w_down_s,
                  L0_c, L0_wmod):
    f = np.float32
    x = np.ascontiguousarray(np.asarray(L0_attn_x, dtype=f))
    q_w = np.asarray(L0_attn_q_w, f)
    q_b = np.asarray(L0_attn_q_b, f)
    kv_w = np.asarray(L0_attn_kv_w, f)
    kv_b = np.asarray(L0_attn_kv_b, f)
    o_w = np.asarray(L0_attn_o_w, f)
    ow_c = np.ascontiguousarray(o_w.reshape(CC, 128, C).transpose(1, 0, 2))
    o_b = np.asarray(L0_attn_o_b, f)
    up = np.asarray(L0_ffn_up, f)
    gate = np.asarray(L0_ffn_gate, f)
    down = np.asarray(L0_ffn_down, f)
    router = np.asarray(L0_ffn_router, f)
    wus = np.asarray(L0_ffn_w_up_s, f)
    wgs = np.asarray(L0_ffn_w_gate_s, f)
    wds = np.asarray(L0_ffn_w_down_s, f)
    c_ = np.asarray(L0_c, f)
    wm = np.asarray(L0_wmod, f)
    wm = np.ascontiguousarray(wm.reshape(CC, 128, 12, 512).transpose(1, 2, 0, 3))

    qw_blk = np.ascontiguousarray(q_w.reshape(CC, 128, HPAIRS, 128).transpose(2, 1, 0, 3))
    qb_col = np.ascontiguousarray(q_b.reshape(8, 128).T)
    kvw_blk = np.stack([np.concatenate([kv_w[:, 128 * hp:128 * hp + 128],
                                        kv_w[:, C + 128 * hp:C + 128 * hp + 128]], axis=1)
                        for hp in range(HPAIRS)])
    kvw_blk = np.ascontiguousarray(kvw_blk.reshape(HPAIRS, CC, 128, 256).transpose(0, 2, 1, 3))
    bvr = np.ascontiguousarray(kv_b[C:2 * C][None, :])
    ob_col = np.ascontiguousarray(o_b.reshape(8, 128).T)
    up_all = np.concatenate([up, wus[None]], axis=0)
    ga_all = np.concatenate([gate, wgs[None]], axis=0)
    dn_all = np.concatenate([down, wds[None]], axis=0)
    upt = np.ascontiguousarray(
        up_all.reshape(NE, HHN, 128, CC, 128).transpose(0, 1, 4, 3, 2))
    gat = np.ascontiguousarray(
        ga_all.reshape(NE, HHN, 128, CC, 128).transpose(0, 1, 4, 3, 2))
    dnt = np.ascontiguousarray(
        dn_all.transpose(0, 2, 1).reshape(NE, HHN, 128, C).transpose(0, 2, 1, 3))
    rt = np.ascontiguousarray(router.T)
    ow_c = np.ascontiguousarray(o_w.reshape(CC, 128, C).transpose(1, 0, 2))

    in_maps = []
    for core in range(8):
        b, par = core // 2, core % 2
        xs = np.zeros((C, TL), f)
        sh = PAD - 512 * par
        lo, hi = sh, min(TL, sh + TSEQ)
        xs[:, lo:hi] = x[b].T[:, 0:hi - lo]
        xs = np.ascontiguousarray(xs.reshape(CC, 128, TL // 512, 512).transpose(1, 2, 0, 3))
        valid = np.zeros((128, 48), f)
        uu = np.arange(48 * 128).reshape(48, 128).T
        g = uu - sh
        valid[(g >= 0) & (g < TSEQ)] = 1.0
        cb_col = np.ascontiguousarray(c_[b].reshape(8, 128).T)
        in_maps.append({
            "x": np.ascontiguousarray(xs), "valid": np.ascontiguousarray(valid),
            "qw": qw_blk, "qb": qb_col, "kvw": kvw_blk, "bvr": bvr,
            "ow": ow_c, "ob": ob_col, "upt": upt, "gat": gat, "dnt": dnt,
            "rt": rt, "wm": wm, "cb": cb_col,
        })
    return in_maps


def kernel(**inputs):
    in_maps = _stage_inputs(**inputs)
    nc = build_nc("full")
    res = run_bass_kernel_spmd(nc, in_maps, core_ids=list(range(8)))
    out = np.zeros((4, TSEQ, C), np.float32)
    for core in range(8):
        b, par = core // 2, core % 2
        oc = res.results[core]["out"]
        for i in range(OWN):
            g0 = (par + 2 * i) * 512
            out[b, g0:g0 + 512, :] = oc[:, 512 * i:512 * i + 512].T
    return out
